# version 07
# baseline (speedup 1.0000x reference)
import contextlib
import numpy as np
import ml_dtypes

import concourse.bass as bass
import concourse.mybir as mybir
from concourse.bass_utils import run_bass_kernel_spmd

F32 = mybir.dt.float32
BF16 = mybir.dt.bfloat16
AF = mybir.ActivationFunctionType
ALU = mybir.AluOpType

P = 128
NT = 8
NPT = 24
H = 16
C = 2048
NCH = 16
NE = 8
M = 1024
NHC = 8
KB = 1024
KiB = 1024

DEBUG_TAPS = False


class Tracker:
    def __init__(self, nc, stack):
        self.nc = nc
        self.count = {}
        self.handle = {}
        self.stack = stack

    def sem(self, name):
        self.handle[name] = self.stack.enter_context(self.nc.semaphore(name))
        self.count[name] = 0
        return name

    def inc(self, inst, s, n=1):
        self.count[s] += n
        inst.then_inc(self.handle[s], n)
        return self.count[s]

    def wait(self, eng, s, val=None):
        v = self.count[s] if val is None else val
        assert v <= self.count[s], f"wait on {s} for future value {v} > {self.count[s]}"
        if v > 0:
            eng.wait_ge(self.handle[s], v)


def build_program():
    nc = bass.Bass("TRN2", target_bir_lowering=False)
    stack = contextlib.ExitStack()
    t = Tracker(nc, stack)

    d_x = nc.dram_tensor("xs", [KB, C], F32, kind="ExternalInput").ap()
    d_xp = nc.dram_tensor("xp", [NPT * P, C], F32, kind="ExternalInput").ap()
    d_wqkv = nc.dram_tensor("wqkv", [C, 3 * C], F32, kind="ExternalInput").ap()
    d_bqkv = nc.dram_tensor("bqkv", [3 * C], F32, kind="ExternalInput").ap()
    d_wout = nc.dram_tensor("wout", [C, C], F32, kind="ExternalInput").ap()
    d_bout = nc.dram_tensor("bout", [C], F32, kind="ExternalInput").ap()
    d_slopes = nc.dram_tensor("slopes", [H], F32, kind="ExternalInput").ap()
    d_wg = nc.dram_tensor("wg_im", [P, NCH * NE], F32, kind="ExternalInput").ap()
    d_wffn = nc.dram_tensor("wffn_im", [NE * 3 * M, C], F32, kind="ExternalInput").ap()
    d_ident = nc.dram_tensor("c_ident", [P, P], BF16, kind="ExternalInput").ap()
    d_tri = nc.dram_tensor("c_tri", [P, P], F32, kind="ExternalInput").ap()
    d_cols = nc.dram_tensor("c_cols", [P, 3], F32, kind="ExternalInput").ap()
    d_tsc = nc.dram_tensor("c_tsc", [P, NT], F32, kind="ExternalInput").ap()
    d_ptsc = nc.dram_tensor("c_ptsc", [P, NPT], F32, kind="ExternalInput").ap()
    d_rows = nc.dram_tensor("c_rows", [1, 2 * P], F32, kind="ExternalInput").ap()
    d_y = nc.dram_tensor("y", [KB, C], F32, kind="ExternalOutput").ap()

    d_wqkvb = nc.dram_tensor("wqkvb", [C, 3 * C], BF16, kind="Internal").ap()
    d_wall = nc.dram_tensor("wall", [NE * 3 * M, C], BF16, kind="Internal").ap()
    d_woutb = nc.dram_tensor("woutb", [C, C], BF16, kind="Internal").ap()
    d_x1 = nc.dram_tensor("x1d", [KB, C], F32, kind="Internal").ap()
    d_biasb = nc.dram_tensor("biasb", [1, 4 * C], BF16, kind="Internal").ap()

    taps = {}
    if DEBUG_TAPS:
        taps["tap_g"] = nc.dram_tensor("tap_g", [P, NT * NE], F32, kind="ExternalOutput").ap()
        taps["tap_x1"] = nc.dram_tensor("tap_x1", [KB, C], F32, kind="ExternalOutput").ap()
        taps["tap_S0"] = nc.dram_tensor("tap_S0", [P, H * P], F32, kind="ExternalOutput").ap()

    B = 18 * KiB

    def at(name, shape, dtype, off):
        assert off % 32 == 0
        return nc.alloc_sbuf_tensor_at(name, shape, dtype, offset=B + off)

    o = 0

    def nxt(name, shape, dtype):
        nonlocal o
        h = at(name, shape, dtype, o)
        sz = int(np.prod(shape[1:])) * (2 if dtype == BF16 else 4)
        o += (sz + 31) // 32 * 32
        return h

    s_decay = nxt("s_decay", [P, H * P], F32)
    s_bout = nxt("s_bout", [1, 2 * KB], BF16)
    s_qdec = nxt("s_qdec", [P, H], F32)
    s_kdec = nxt("s_kdec", [P, H], F32)
    s_dcoef = nxt("s_dcoef", [P, H], F32)
    s_negs = nxt("s_negs", [P, H], F32)
    s_tri = nxt("s_tri", [P, P], F32)
    s_cols = nxt("s_cols", [P, 4], F32)
    s_tsc = nxt("s_tsc", [P, NT], F32)
    s_ptsc = nxt("s_ptsc", [P, NPT], F32)
    s_rs = nxt("s_rs", [P, NT], F32)
    s_ident = nxt("s_ident", [P, P], BF16)
    s_rows32 = nxt("s_rows32", [1, 2 * P], F32)
    s_srow = nxt("s_srow", [1, P], F32)
    s_nsrow = nxt("s_nsrow", [1, P], F32)
    s_wg = nxt("s_wg", [P, NCH * NE], BF16)
    s_Sbf = nxt("s_Sbf", [P, 2 * P], BF16)
    s_qT = nxt("s_qT", [P, 2 * P], BF16)
    s_qdT = nxt("s_qdT", [P, 2 * P], BF16)
    s_kT = nxt("s_kT", [P, 2 * P], BF16)
    s_scm = nxt("s_scm", [P, 2 * P], BF16)
    s_kbd = nxt("s_kbd", [P, 2 * P], BF16)
    s_qbd = nxt("s_qbd", [P, 2 * P], BF16)
    s_scr32 = at("s_scr32", [P, KB], F32, 85 * KiB)
    s_silu = at("s_silu", [P, 2 * KB], BF16, 173 * KiB)
    s_dmm = s_scr32.ap()[:, 0:P]
    s_wgf = s_scr32.ap()[:, P:2 * P]
    s_stat = nxt("s_stat", [P, 48], F32)
    s_aggr = nxt("s_aggr", [P, 4], F32)
    s_rstd = nxt("s_rstd", [P, 2], F32)
    s_nmr = nxt("s_nmr", [P, 2], F32)
    s_eps = nxt("s_eps", [P, 1], F32)
    s_lgt = nxt("s_lgt", [P, NE], F32)
    s_max8 = nxt("s_max8", [P, NE], F32)
    s_dx1 = nxt("s_dx1", [P, 1], F32)
    s_w1 = nxt("s_w1", [P, 1], F32)
    s_w2 = nxt("s_w2", [P, 1], F32)
    s_g1 = nxt("s_g1", [P, NE], F32)
    s_g2 = nxt("s_g2", [P, NE], F32)
    s_g = nxt("s_g", [P, NT * NE], F32)
    s_onesb = nxt("s_onesb", [1, P], BF16)
    s_onesf = nxt("s_onesf", [1, P], F32)
    s_S0 = nxt("s_S0", [P, H * P], F32)
    assert o <= 29 * KiB, o

    s_lnT = at("s_lnT", [P, NCH * KB], BF16, 29 * KiB)
    s_q = at("s_q", [P, NT * C], BF16, 61 * KiB)
    s_k = at("s_k", [P, NT * C], BF16, 93 * KiB)
    s_v = at("s_v", [P, NT * C], BF16, 125 * KiB)
    s_cvf = at("s_cvf", [P, 2 * C], F32, 61 * KiB)
    s_cvb = at("s_cvb", [P, 2 * C], BF16, 77 * KiB)
    s_wqcf = at("s_wqcf", [P, 2 * 3 * C], F32, 61 * KiB)
    s_wqcb = at("s_wqcb", [P, 2 * 3 * C], BF16, 125 * KiB)
    s_bst = at("s_bst", [1, C], F32, 109 * KiB)
    s_rowsbf = at("s_rowsbf", [1, C], BF16, 117 * KiB)
    s_kvst = at("s_kvst", [P, 4 * 4096], BF16, 93 * KiB)
    s_lntile = at("s_lntile", [P, 2 * C], BF16, 85 * KiB)
    s_x1t = at("s_x1t", [P, 2 * C], F32, 93 * KiB)
    s_wostr = at("s_wostr", [P, 2 * C], BF16, 109 * KiB)
    s_out = at("s_out", [P, NT * C], F32, 61 * KiB)
    s_wo = at("s_wo", [P, 2 * NHC * KB], BF16, 125 * KiB)
    s_wcg = at("s_wcg", [P, NCH * KB], BF16, 157 * KiB)
    s_aT = at("s_aT", [P, H * KB], BF16, 157 * KiB)
    s_inter = at("s_inter", [P, NHC * KB], BF16, 157 * KiB)
    s_cvf2 = at("s_cvf2", [P, 2 * C], F32, 29 * KiB)
    s_cvb2 = at("s_cvb2", [P, 2 * C], BF16, 45 * KiB)
    s_xstr = at("s_xstr", [P, 2 * C], F32, 189 * KiB)
    s_wi = at("s_wi", [P, 4 * C], BF16, 189 * KiB)

    pbs = [stack.enter_context(nc.psum_tensor(f"pb{i}", [P, 512], F32)) for i in range(8)]

    def pbank(i, n=512, off=0):
        return pbs[i].ap()[:, off:off + n]

    def pbankb(i, n=1024, off=0):
        return pbs[i].ap().bitcast(BF16)[:, off:off + n]

    pe = t.sem("pe"); av = t.sem("av"); dv = t.sem("dv")
    dC = t.sem("dC")
    dXp = [t.sem("dX0"), t.sem("dX1")]
    dWp = [t.sem("dW0"), t.sem("dW1")]
    dQp = [t.sem("dQ0"), t.sem("dQ1")]
    dQs = [t.sem("dQs0"), t.sem("dQs1")]
    dEp = [t.sem("dE0"), t.sem("dE1")]
    dDp = [t.sem("dD0"), t.sem("dD1")]
    dVw = [t.sem("dVw0"), t.sem("dVw1")]
    dFp = [t.sem("dF0"), t.sem("dF1")]
    dGp = [t.sem("dG0"), t.sem("dG1")]
    dOp = [t.sem("dO0"), t.sem("dO1")]
    dS = t.sem("dS")
    dB = t.sem("dB")
    dR = t.sem("dR")

    sync, scal, vec, ten = nc.sync, nc.scalar, nc.vector, nc.tensor


    def dma2(eng, out_ap, in_ap, sem, cols, nsplit):
        step = cols // nsplit
        for k in range(nsplit):
            t.inc(eng.dma_start(out=out_ap[:, k * step:(k + 1) * step],
                                in_=in_ap[:, k * step:(k + 1) * step]), sem, 16)
        return t.count[sem]

    t.inc(sync.dma_start(out=s_ident[:], in_=d_ident), dC, 16)
    t.inc(sync.dma_start(out=s_tri[:], in_=d_tri), dC, 16)
    t.inc(sync.dma_start(out=s_cols[:, 0:3], in_=d_cols), dC, 16)
    t.inc(sync.dma_start(out=s_tsc[:], in_=d_tsc), dC, 16)
    t.inc(sync.dma_start(out=s_ptsc[:], in_=d_ptsc), dC, 16)
    t.inc(sync.dma_start(out=s_rows32[:], in_=d_rows), dC, 16)
    t.inc(sync.dma_start(out=s_wgf, in_=d_wg), dC, 16)
    for r in range(8):
        t.inc(sync.dma_start(out=s_srow[0:1, r * H:(r + 1) * H], in_=d_slopes[None, :]), dC, 16)
    e_dc = t.count[dC]
    bias_cvt = 0
    bias_stv = 0
    for bc in range(4):
        if bc > 0:
            t.wait(sync, av, bias_cvt)
        if bc < 3:
            t.inc(sync.dma_start(out=s_bst[0:1, :], in_=d_bqkv[None, bc * C:(bc + 1) * C]), dB, 16)
        else:
            t.inc(sync.dma_start(out=s_bst[0:1, :], in_=d_bout[None, :]), dB, 16)
        t.wait(scal, dB, t.count[dB])
        t.inc(scal.copy(s_rowsbf[0:1, :], s_bst[0:1, :]), av)
        bias_cvt = t.count[av]
        t.wait(sync, av, bias_cvt)
        t.inc(sync.dma_start(out=d_biasb[0:1, bc * C:(bc + 1) * C], in_=s_rowsbf[0:1, :]), dB, 16)
        bias_stv = t.count[dB]
    e_bias_bf = bias_cvt
    e_bias_st = bias_stv
    t.wait(vec, dC, e_dc)
    t.inc(vec.tensor_copy(s_wg[:], s_wgf), dv)
    e_wg = t.count[dv]
    t.inc(vec.memset(s_eps[:], 1e-5), dv)
    t.inc(vec.memset(s_onesb[:], 1.0), dv)
    t.inc(vec.memset(s_onesf[:], 1.0), dv)
    t.inc(vec.tensor_scalar(s_nsrow[0:1, 0:H], s_srow[0:1, 0:H], -1.0, None, ALU.mult), dv)
    e_setup_dv = t.count[dv]

    t.wait(ten, dv, e_setup_dv)
    ten.matmul(pbank(0, H), s_onesf[:], s_nsrow[0:1, 0:H], start=True, stop=True)
    e_bc = t.inc(ten.matmul(pbank(2, P), s_onesf[:], s_rows32[0:1, 0:P], start=True, stop=True), pe)
    t.wait(vec, pe, e_bc)
    t.inc(vec.tensor_copy(s_negs[:], pbank(0, H)), dv)
    t.inc(vec.tensor_scalar(s_dmm, pbank(2, P), s_cols[:, 0:1], None, ALU.subtract), dv)
    t.wait(vec, dv, t.count[dv])
    t.inc(vec.tensor_mul(s_dmm, s_dmm, s_tri[:]), dv)
    e_dmm = t.count[dv]

    t.wait(scal, dv, e_dmm)
    for h in range(H):
        t.inc(scal.activation(s_decay[:, h * P:(h + 1) * P], s_dmm, AF.Exp,
                              scale=s_negs[:, h:h + 1]), av)
        t.inc(scal.activation(s_qdec[:, h:h + 1], s_cols[:, 0:1], AF.Exp,
                              scale=s_negs[:, h:h + 1]), av)
        t.inc(scal.activation(s_kdec[:, h:h + 1], s_cols[:, 1:2], AF.Exp,
                              scale=s_negs[:, h:h + 1]), av)
        t.inc(scal.activation(s_dcoef[:, h:h + 1], s_cols[:, 2:3], AF.Exp,
                              scale=s_negs[:, h:h + 1]), av)
    e_decay_exp = t.count[av]
    t.wait(vec, av, e_decay_exp)
    for h in range(H):
        t.inc(vec.tensor_mul(s_decay[:, h * P:(h + 1) * P],
                             s_decay[:, h * P:(h + 1) * P], s_tri[:]), dv)
    e_decay = t.count[dv]

    wq_cvt = {}
    wq_st = {}
    for i in range(NCH):
        par = i % 2
        if i >= 2:
            t.wait(sync, dv, wq_cvt[i - 2])
        dma2(sync, s_wqcf[:, par * 3 * C:(par + 1) * 3 * C],
             d_wqkv[i * P:(i + 1) * P, :], dQp[par], 3 * C, 6)
        t.wait(vec, dQp[par], t.count[dQp[par]])
        if i >= 2:
            t.wait(vec, dQs[par], wq_st[i - 2])
        t.inc(vec.tensor_copy(s_wqcb[:, par * 3 * C:(par + 1) * 3 * C],
                              s_wqcf[:, par * 3 * C:(par + 1) * 3 * C]), dv)
        wq_cvt[i] = t.count[dv]
        t.wait(sync, dv, wq_cvt[i])
        for k in range(3):
            t.inc(sync.dma_start(out=d_wqkvb[i * P:(i + 1) * P, k * 2 * KB:(k + 1) * 2 * KB],
                                 in_=s_wqcb[:, par * 3 * C + k * 2 * KB: par * 3 * C + (k + 1) * 2 * KB]),
                  dQs[par], 16)
        wq_st[i] = t.count[dQs[par]]
    e_wq_pe_gate = t.count[dv]

    ffn_cv = {"i": 0, "rd": {}, "st": {}}

    def emit_ffn_conv(n):
        for _ in range(n):
            i = ffn_cv["i"]
            if i >= NE * 3 * NHC:
                return
            ffn_cv["i"] = i + 1
            par = i % 2
            if i >= 2:
                t.wait(scal, dv, ffn_cv["rd"][i - 2])
            elif i == 0:
                t.wait(scal, dv, e_wq_pe_gate)
            dma2(scal, s_cvf[:, par * C:(par + 1) * C], d_wffn[i * P:(i + 1) * P, :],
                 dEp[par], C, 2)
            t.wait(vec, dEp[par], t.count[dEp[par]])
            if i >= 2:
                t.wait(vec, dDp[par], ffn_cv["st"][i - 2])
            t.inc(vec.tensor_copy(s_cvb[:, par * C:(par + 1) * C],
                                  s_cvf[:, par * C:(par + 1) * C]), dv)
            ffn_cv["rd"][i] = t.count[dv]
            t.wait(scal, dv, ffn_cv["rd"][i])
            for k in range(2):
                t.inc(scal.dma_start(out=d_wall[i * P:(i + 1) * P, k * KB:(k + 1) * KB],
                                     in_=s_cvb[:, par * C + k * KB: par * C + (k + 1) * KB]),
                      dDp[par], 16)
            ffn_cv["st"][i] = t.count[dDp[par]]

    emit_ffn_conv(24)

    ln_state = {"stats_dv": {}, "apply": {}, "tp": 0}

    def emit_ln_tile(xsl, par, prev_pe_gate):
        stt = s_stat[:, par * 24:par * 24 + 24]
        agg = s_aggr[:, par * 2:par * 2 + 2]
        rst = s_rstd[:, par:par + 1]
        nmr = s_nmr[:, par:par + 1]
        for g in range(4):
            t.inc(vec.bn_stats(stt[:, 6 * g:6 * g + 6], xsl[:, 512 * g:512 * (g + 1)]), dv)
        t.wait(vec, dv, t.count[dv])
        t.inc(vec.bn_aggr(agg, stt), dv)
        e_ag = t.count[dv]
        t.wait(scal, dv, e_ag)
        t.inc(scal.activation(rst, agg[:, 1:2], AF.Sqrt, bias=s_eps[:]), av)
        t.wait(vec, av, t.count[av])
        t.inc(vec.reciprocal(rst, rst), dv)
        t.wait(vec, dv, t.count[dv])
        t.inc(vec.tensor_scalar(nmr, agg[:, 0:1], rst, -1.0, ALU.mult, ALU.mult), dv)
        e_nmr = t.count[dv]
        t.wait(scal, dv, e_nmr)
        if prev_pe_gate:
            t.wait(scal, pe, prev_pe_gate)
        t.inc(scal.activation(s_lntile[:, par * C:(par + 1) * C], xsl, AF.Identity,
                              bias=nmr, scale=rst), av)
        return t.count[dv], t.count[av]

    def emit_transposes(par, slot, banks=(0, 1)):
        t.wait(ten, av, t.count[av])
        for ch in range(NCH):
            bk = banks[ch % 2]
            if ln_state["tp"]:
                t.wait(ten, av, ln_state["tp"])
            t.inc(ten.transpose(pbankb(bk, P), s_lntile[:, par * C + ch * P: par * C + (ch + 1) * P],
                                s_ident[:]), pe)
            t.wait(scal, pe, t.count[pe])
            ln_state["tp"] = t.inc(scal.copy(s_lnT[:, ch * KB + slot * P: ch * KB + (slot + 1) * P],
                                             pbankb(bk, P)), av)
        return t.count[pe], t.count[av]

    t.wait(vec, dv, e_decay)
    t.inc(vec.memset(s_S0[:], 0.0), dv)
    e_s0_ms = t.count[dv]
    wcg_user = {}
    wcg_ld = {}
    ub_dv = {}
    e_prev_u_dv = 0
    brow_ld = 0
    x_ev = {}
    pfx_lnT_pe = 0
    pfx_u_mm = {}
    e_prev_kv_mm = 0
    e_prefix_pe = 0
    sbf_gate = 0
    for g in range(6):
        for pt4 in range(4):
            pt = g * 4 + pt4
            par = pt % 2
            if pt >= 2:
                t.wait(sync, av, ln_state["apply"].get(pt - 2, 0))
                t.wait(sync, dv, ln_state["stats_dv"].get(pt - 2, 0))
            dma2(sync, s_xstr[:, par * C:(par + 1) * C], d_xp[pt * P:(pt + 1) * P, :],
                 dXp[par], C, 4)
            x_ev[pt] = t.count[dXp[par]]
            t.wait(vec, dXp[par], x_ev[pt])
            if pt == 0:
                t.wait(vec, dv, e_s0_ms)
            sd, ap = emit_ln_tile(s_xstr[:, par * C:(par + 1) * C], par,
                                  pfx_lnT_pe if pt >= 2 else 0)
            ln_state["stats_dv"][pt] = sd
            ln_state["apply"][pt] = ap
            pfx_lnT_pe, _ = emit_transposes(par, pt4)
        emit_ffn_conv(28)
        for cg in range(4):
            colbase = C + cg * KB
            t.wait(sync, dB, e_bias_st)
            if brow_ld:
                t.wait(sync, pe, brow_pe_gate)
            t.inc(sync.dma_start(out=s_bout[0:1, 0:KB], in_=d_biasb[0:1, colbase:colbase + KB]), dR, 16)
            brow_ld = t.count[dR]
            for hf in range(2):
                if (g, cg) != (0, 0):
                    t.wait(sync, pe, wcg_user.get(0, 0))
                elif hf == 0:
                    t.wait(sync, dQs[0], wq_st[NCH - 2])
                    t.wait(sync, dQs[1], wq_st[NCH - 1])
                for kk in range(4):
                    src = d_wqkvb[hf * KB + kk * 2 * P: hf * KB + (kk + 1) * 2 * P,
                                  colbase:colbase + KB].rearrange("(c p) n -> p c n", p=P)
                    dst = s_wcg[:, (hf * 8 + 2 * kk) * KB:(hf * 8 + 2 * kk + 2) * KB].rearrange(
                        "p (c n) -> p c n", n=KB)
                    t.inc(sync.dma_start(out=dst, in_=src), dWp[hf], 16)
                wcg_ld[hf] = t.count[dWp[hf]]
            t.wait(ten, av, t.count[av])
            if e_prev_kv_mm:
                t.wait(ten, av, kv_evac_ev)
            if e_prev_u_dv:
                t.wait(ten, dv, e_prev_u_dv)
            t.wait(ten, dWp[0], wcg_ld[0])
            t.wait(ten, dWp[1], wcg_ld[1])
            for i in range(NCH):
                for pt4 in range(4):
                    lhs = s_lnT[:, i * KB + pt4 * P: i * KB + (pt4 + 1) * P]
                    st_ = (i == 0)
                    ten.matmul(pbank(2 * pt4, 512), lhs, s_wcg[:, i * KB:i * KB + 512],
                               start=st_, stop=False)
                    ten.matmul(pbank(2 * pt4 + 1, 512), lhs, s_wcg[:, i * KB + 512:(i + 1) * KB],
                               start=st_, stop=False)
            t.wait(ten, dR, brow_ld)
            for pt4 in range(4):
                ten.matmul(pbank(2 * pt4, 512), s_onesb[:], s_bout[0:1, 0:512],
                           start=False, stop=True)
                mm = ten.matmul(pbank(2 * pt4 + 1, 512), s_onesb[:], s_bout[0:1, 512:KB],
                                start=False, stop=True)
            e_prev_kv_mm = t.inc(mm, pe)
            brow_pe_gate = e_prev_kv_mm
            wcg_user[0] = e_prev_kv_mm
            t.wait(scal, pe, e_prev_kv_mm)
            if g == 0 and cg == 0:
                t.wait(scal, dv, e_wg)
            for pt4 in range(4):
                base = pt4 * 4096 + cg * KB
                t.inc(scal.copy(s_kvst[:, base:base + 512], pbank(2 * pt4, 512)), av)
                t.inc(scal.copy(s_kvst[:, base + 512:base + KB], pbank(2 * pt4 + 1, 512)), av)
            kv_evac_ev = t.count[av]
        t.wait(ten, av, kv_evac_ev)
        for h in range(H):
            par2 = h % 2
            t.wait(scal, av, t.count[av])
            t.inc(scal.activation(s_rs[:, 0:4], s_ptsc[:, g * 4:(g + 1) * 4], AF.Exp,
                                  scale=s_negs[:, h:h + 1]), av)
            t.wait(scal, av, t.count[av])
            if pfx_u_mm.get(par2):
                t.wait(scal, pe, pfx_u_mm[par2])
            for pt4 in range(4):
                dstsl = (s_kbd, s_qbd, s_qT, s_qdT)[pt4][:, par2 * P:(par2 + 1) * P]
                t.inc(scal.activation(dstsl,
                                      s_kvst[:, pt4 * 4096 + h * P: pt4 * 4096 + (h + 1) * P],
                                      AF.Identity, scale=s_rs[:, pt4:pt4 + 1]), av)
            e_ksc = t.count[av]
            t.wait(ten, av, e_ksc)
            if par2 in ub_dv:
                t.wait(ten, dv, ub_dv[par2])
            for pt4 in range(4):
                srcsl = (s_kbd, s_qbd, s_qT, s_qdT)[pt4][:, par2 * P:(par2 + 1) * P]
                mm = ten.matmul(pbank(4 + par2, P), srcsl,
                                s_kvst[:, pt4 * 4096 + 2048 + h * P: pt4 * 4096 + 2048 + (h + 1) * P],
                                start=(pt4 == 0), stop=(pt4 == 3))
            pfx_u_mm[par2] = t.inc(mm, pe)
            t.wait(vec, pe, pfx_u_mm[par2])
            t.wait(vec, dv, t.count[dv])
            t.inc(vec.tensor_tensor(s_S0[:, h * P:(h + 1) * P], s_S0[:, h * P:(h + 1) * P],
                                    pbank(4 + par2, P), ALU.add), dv)
            ub_dv[par2] = t.count[dv]
        e_prev_u_dv = t.count[dv]
        e_prefix_pe = t.count[pe]
        emit_ffn_conv(2)
    e_S0 = t.count[dv]
    e_prefix_av = t.count[av]

    ln_state["tp"] = ln_state["tp"]
    lnT_ev = {}
    own_apply = {}
    for tt in range(NT):
        par = tt % 2
        if tt < 2:
            t.wait(sync, av, ln_state["apply"][NPT - 2 + tt])
            t.wait(sync, dv, ln_state["stats_dv"][NPT - 2 + tt])
        else:
            t.wait(sync, av, own_apply[tt - 2])
            t.wait(sync, dv, own_stats[tt - 2])
        dma2(sync, s_xstr[:, par * C:(par + 1) * C], d_x[tt * P:(tt + 1) * P, :],
             dXp[par], C, 4)
        xev = t.count[dXp[par]]
        t.wait(vec, dXp[par], xev)
        sd, ap = emit_ln_tile(s_xstr[:, par * C:(par + 1) * C], par,
                              lnT_ev[tt - 2][0] if tt >= 2 else e_prefix_pe)
        if tt == 0:
            own_stats = {}
        own_stats[tt] = sd
        own_apply[tt] = ap
        lnT_ev[tt] = emit_transposes(par, tt)
    e_ln1_av = t.count[av]
    e_ln1_dv = t.count[dv]

    dest = [(s_q, 0), (s_q, 1), (s_k, 0), (s_k, 1), (s_v, 0), (s_v, 1)]
    qkv_mm_done = {}
    qkv_evac = {}
    for cg in range(6):
        t.wait(sync, dB, e_bias_st)
        t.wait(sync, pe, brow_pe_gate)
        t.inc(sync.dma_start(out=s_bout[0:1, 0:KB], in_=d_biasb[0:1, cg * KB:(cg + 1) * KB]), dR, 16)
        brow_ld = t.count[dR]
        for hf in range(2):
            t.wait(sync, pe, wcg_user[0])
            for kk in range(4):
                src = d_wqkvb[hf * KB + kk * 2 * P: hf * KB + (kk + 1) * 2 * P,
                              cg * KB:(cg + 1) * KB].rearrange("(c p) n -> p c n", p=P)
                dst = s_wcg[:, (hf * 8 + 2 * kk) * KB:(hf * 8 + 2 * kk + 2) * KB].rearrange(
                    "p (c n) -> p c n", n=KB)
                t.inc(sync.dma_start(out=dst, in_=src), dWp[hf], 16)
            wcg_ld[hf] = t.count[dWp[hf]]
        for grp in range(2):
            tts = list(range(grp * 4, grp * 4 + 4))
            t.wait(ten, av, e_ln1_av if cg == 0 else lnT_ev[max(tts)][1])
            if grp == 1:
                t.wait(ten, av, qkv_evac[(cg, 0)])
            elif cg > 0:
                t.wait(ten, av, qkv_evac[(cg - 1, 1)])
            else:
                t.wait(ten, av, e_prefix_av)
                t.wait(ten, dv, e_S0)
            t.wait(ten, dWp[0], wcg_ld[0])
            t.wait(ten, dWp[1], wcg_ld[1])
            for i in range(NCH):
                for j, tt in enumerate(tts):
                    lhs = s_lnT[:, i * KB + tt * P: i * KB + (tt + 1) * P]
                    st_ = (i == 0)
                    ten.matmul(pbank(2 * j, 512), lhs, s_wcg[:, i * KB:i * KB + 512],
                               start=st_, stop=False)
                    ten.matmul(pbank(2 * j + 1, 512), lhs, s_wcg[:, i * KB + 512:(i + 1) * KB],
                               start=st_, stop=False)
            t.wait(ten, dR, brow_ld)
            for j, tt in enumerate(tts):
                ten.matmul(pbank(2 * j, 512), s_onesb[:], s_bout[0:1, 0:512],
                           start=False, stop=True)
                mm = ten.matmul(pbank(2 * j + 1, 512), s_onesb[:], s_bout[0:1, 512:KB],
                                start=False, stop=True)
            qkv_mm_done[(cg, grp)] = t.inc(mm, pe)
            if grp == 1:
                brow_pe_gate = qkv_mm_done[(cg, grp)]
                wcg_user[0] = qkv_mm_done[(cg, grp)]
            t.wait(scal, pe, qkv_mm_done[(cg, grp)])
            if cg == 0 and grp == 0:
                t.wait(scal, dEp[0], t.count["dE0"])
                t.wait(scal, dEp[1], t.count["dE1"])
                t.wait(scal, dDp[0], t.count["dD0"])
                t.wait(scal, dDp[1], t.count["dD1"])
                t.wait(scal, dQs[0], t.count["dQs0"])
                t.wait(scal, dQs[1], t.count["dQs1"])
                t.wait(scal, dVw[0], t.count["dVw0"])
                t.wait(scal, dVw[1], t.count["dVw1"])
            dst, half = dest[cg]
            for j, tt in enumerate(tts):
                base = tt * C + half * KB
                t.inc(scal.copy(dst[:, base:base + 512], pbank(2 * j, 512)), av)
                t.inc(scal.copy(dst[:, base + 512:base + KB], pbank(2 * j + 1, 512)), av)
            qkv_evac[(cg, grp)] = t.count[av]
        emit_ffn_conv(4)
    e_qkv_av = t.count[av]
    e_qkv_pe = t.count[pe]

    wo_st = {}
    wo_cvt = {}
    for i in range(NCH):
        par = i % 2
        if i >= 2:
            t.wait(sync, dv, wo_cvt[i - 2])
        else:
            t.wait(sync, pe, qkv_mm_done[(5, 1)])
        dma2(sync, s_cvf2[:, par * C:(par + 1) * C], d_wout[i * P:(i + 1) * P, :],
             dQp[par], C, 2)
        t.wait(vec, dQp[par], t.count[dQp[par]])
        if i >= 2:
            t.wait(vec, dVw[par], wo_st[i - 2])
        t.inc(vec.tensor_copy(s_cvb2[:, par * C:(par + 1) * C],
                              s_cvf2[:, par * C:(par + 1) * C]), dv)
        wo_cvt[i] = t.count[dv]
        t.wait(sync, dv, wo_cvt[i])
        for k in range(2):
            t.inc(sync.dma_start(out=d_woutb[i * P:(i + 1) * P, k * KB:(k + 1) * KB],
                                 in_=s_cvb2[:, par * C + k * KB: par * C + (k + 1) * KB]),
                  dVw[par], 16)
        wo_st[i] = t.count[dVw[par]]
    e_woutcv_dv = t.count[dv]

    last_scm = {0: 0, 1: 0}
    last_oT_evac = {0: 0, 1: 0}
    last_supd_add = {0: 0, 1: 0}
    qbd_tp_mm = {0: 0, 1: 0}
    supd_mm = {0: 0, 1: 0}
    oT_mm_ev = {0: 0, 1: 0}
    sbf_ev = {0: 0, 1: 0}
    last_tp_evac = {4: 0, 5: 0}
    SC = {0: 0, 1: 2}
    OT = {0: 1, 1: 7}
    SU = {0: 3, 1: 6}
    tpbank = [0]
    for hp in range(H // 2):
        for j in (0, 1):
            h = 2 * hp + j
            t.wait(vec, dv, t.count[dv])
            if oT_mm_ev[j]:
                t.wait(vec, pe, oT_mm_ev[j])
            elif hp == 0:
                t.wait(vec, dv, e_S0)
            t.inc(vec.tensor_copy(s_Sbf[:, j * P:(j + 1) * P], s_S0[:, h * P:(h + 1) * P]), dv)
            sbf_ev[j] = t.count[dv]
        for c in range(NT):
            for j in (0, 1):
                h = 2 * hp + j
                qt = s_q[:, c * C + h * P: c * C + (h + 1) * P]
                kt = s_k[:, c * C + h * P: c * C + (h + 1) * P]
                vt = s_v[:, c * C + h * P: c * C + (h + 1) * P]
                if hp == 0 and c == 0 and j == 0:
                    t.wait(scal, av, e_qkv_av)
                    t.wait(scal, pe, e_qkv_pe)
                if qbd_tp_mm[j]:
                    t.wait(scal, pe, qbd_tp_mm[j])
                t.inc(scal.activation(s_qbd[:, j * P:(j + 1) * P], qt, AF.Identity,
                                      scale=s_qdec[:, h:h + 1]), av)
                qbd_ev = t.count[av]
                if supd_mm[j]:
                    t.wait(scal, pe, supd_mm[j])
                t.inc(scal.activation(s_kbd[:, j * P:(j + 1) * P], kt, AF.Identity,
                                      scale=s_kdec[:, h:h + 1]), av)
                kbd_ev = t.count[av]
                if hp == 0 and c == 0 and j == 0:
                    t.wait(ten, av, e_qkv_av)
                    t.wait(ten, dv, e_S0)
                for which, src2, dst_sb, dep_av in (
                        (0, qt, s_qT, 0), (1, None, s_qdT, qbd_ev), (2, kt, s_kT, 0)):
                    bk = 4 + tpbank[0] % 2
                    tpbank[0] += 1
                    if last_tp_evac[bk]:
                        t.wait(ten, av, last_tp_evac[bk])
                    if dep_av:
                        t.wait(ten, av, dep_av)
                        src2 = s_qbd[:, j * P:(j + 1) * P]
                    mmev = t.inc(ten.transpose(pbankb(bk, P), src2, s_ident[:]), pe)
                    if which == 1:
                        qbd_tp_mm[j] = mmev
                    t.wait(scal, pe, mmev)
                    last_tp_evac[bk] = t.inc(scal.copy(dst_sb[:, j * P:(j + 1) * P],
                                                       pbankb(bk, P)), av)
                e_tp_av = t.count[av]
                t.wait(ten, av, e_tp_av)
                if last_scm[j]:
                    t.wait(ten, dv, last_scm[j])
                sc_mm = t.inc(ten.matmul(pbank(SC[j], P), s_kT[:, j * P:(j + 1) * P],
                                         s_qT[:, j * P:(j + 1) * P], start=True, stop=True), pe)
                t.wait(vec, pe, sc_mm)
                t.inc(vec.tensor_tensor(s_scm[:, j * P:(j + 1) * P], pbank(SC[j], P),
                                        s_decay[:, h * P:(h + 1) * P], ALU.mult), dv)
                last_scm[j] = t.count[dv]
                t.wait(ten, dv, last_scm[j])
                if c == 0:
                    t.wait(ten, dv, sbf_ev[j])
                if last_oT_evac[j]:
                    t.wait(ten, av, last_oT_evac[j])
                ten.matmul(pbank(OT[j], P), vt, s_scm[:, j * P:(j + 1) * P], start=True, stop=False)
                oT_mm_ev[j] = t.inc(ten.matmul(pbank(OT[j], P), s_Sbf[:, j * P:(j + 1) * P],
                                               s_qdT[:, j * P:(j + 1) * P],
                                               start=False, stop=True), pe)
                t.wait(scal, pe, oT_mm_ev[j])
                last_oT_evac[j] = t.inc(scal.copy(s_aT[:, h * KB + c * P: h * KB + (c + 1) * P],
                                                  pbank(OT[j], P)), av)
                if c < NT - 1:
                    t.wait(ten, av, kbd_ev)
                    if last_supd_add[j]:
                        t.wait(ten, dv, last_supd_add[j])
                    supd_mm[j] = t.inc(ten.matmul(pbank(SU[j], P), s_kbd[:, j * P:(j + 1) * P], vt,
                                                  start=True, stop=True), pe)
                    t.wait(vec, pe, oT_mm_ev[j])
                    t.wait(vec, dv, t.count[dv])
                    t.inc(vec.tensor_scalar(s_S0[:, h * P:(h + 1) * P], s_S0[:, h * P:(h + 1) * P],
                                            s_dcoef[:, h:h + 1], None, ALU.mult), dv)
                    t.wait(vec, pe, supd_mm[j])
                    t.wait(vec, dv, t.count[dv])
                    t.inc(vec.tensor_tensor(s_S0[:, h * P:(h + 1) * P], s_S0[:, h * P:(h + 1) * P],
                                            pbank(SU[j], P), ALU.add), dv)
                    last_supd_add[j] = t.count[dv]
                    t.wait(vec, dv, t.count[dv])
                    t.inc(vec.tensor_copy(s_Sbf[:, j * P:(j + 1) * P],
                                          s_S0[:, h * P:(h + 1) * P]), dv)
                    sbf_ev[j] = t.count[dv]
        emit_ffn_conv(2)
    e_scan_pe = t.count[pe]
    e_scan_av = t.count[av]
    e_scan_dv = t.count[dv]
    emit_ffn_conv(1000)
    assert ffn_cv["i"] == NE * 3 * NHC, ffn_cv["i"]
    e_ffncv_st = (t.count["dD0"], t.count["dD1"])

    x1_dma = {}
    resid_ev = {}
    ln2T_ev = {}
    ln2_apply = {}
    g_ev = {}
    wostr_used = {}
    tp2_ev = 0
    lgt_ev = 0
    t.wait(sync, pe, qkv_mm_done[(5, 1)])
    t.inc(sync.dma_start(out=s_bout[0:1, :], in_=d_biasb[0:1, 3 * C:4 * C]), dR, 16)
    e_bout_ld = t.count[dR]
    for tpi in range(4):
        ttA, ttB = 2 * tpi, 2 * tpi + 1
        for j, tt in enumerate((ttA, ttB)):
            par = tt % 2
            if tpi == 0:
                t.wait(sync, av, own_apply[NT - 2 + par])
                t.wait(sync, dv, own_stats[NT - 2 + par])
            else:
                t.wait(sync, dv, resid_ev[tt - 2])
                t.wait(sync, av, ln2_apply[tt - 2])
            dma2(sync, s_xstr[:, par * C:(par + 1) * C], d_x[tt * P:(tt + 1) * P, :],
                 dXp[par], C, 4)
            if j == 0:
                x_evA = t.count[dXp[par]]
            else:
                x_evB = t.count[dXp[par]]
        for i in range(NCH):
            wpar = i % 2
            if tpi == 0 and i < 2:
                t.wait(sync, dVw[0], wo_st[NCH - 2])
                t.wait(sync, dVw[1], wo_st[NCH - 1])
                t.wait(sync, pe, e_scan_pe)
                t.wait(sync, dv, e_scan_dv)
            if (tpi * NCH + i) >= 2:
                t.wait(sync, pe, wostr_used[tpi * NCH + i - 2])
            dma2(sync, s_wostr[:, wpar * C:(wpar + 1) * C], d_woutb[i * P:(i + 1) * P, :],
                 dGp[wpar], C, 2)
            lev = t.count[dGp[wpar]]
            t.wait(ten, dGp[wpar], lev)
            if i == 0:
                t.wait(ten, av, e_scan_av)
                t.wait(ten, dv, e_scan_dv)
                if tpi > 0:
                    t.wait(ten, dv, resid_ev[ttB - 2])
                    t.wait(ten, av, lgt_ev)
            lhsA = s_aT[:, i * KB + ttA * P: i * KB + (ttA + 1) * P]
            lhsB = s_aT[:, i * KB + ttB * P: i * KB + (ttB + 1) * P]
            for b in range(4):
                ten.matmul(pbank(b, 512), lhsA,
                           s_wostr[:, wpar * C + b * 512: wpar * C + (b + 1) * 512],
                           start=(i == 0), stop=False)
            for b in range(4):
                mm = ten.matmul(pbank(4 + b, 512), lhsB,
                                s_wostr[:, wpar * C + b * 512: wpar * C + (b + 1) * 512],
                                start=(i == 0), stop=False)
            wostr_used[tpi * NCH + i] = t.inc(mm, pe)
        t.wait(ten, dR, e_bout_ld)
        for b in range(4):
            ten.matmul(pbank(b, 512), s_onesb[:], s_bout[0:1, b * 512:(b + 1) * 512],
                       start=False, stop=True)
        for b in range(4):
            mm = ten.matmul(pbank(4 + b, 512), s_onesb[:], s_bout[0:1, b * 512:(b + 1) * 512],
                            start=False, stop=True)
        e_pair_mm = t.inc(mm, pe)
        for j, tt in enumerate((ttA, ttB)):
            par = tt % 2
            bset = 4 * j
            t.wait(vec, pe, e_pair_mm)
            t.wait(vec, dXp[par], x_evA if j == 0 else x_evB)
            if tpi > 0:
                t.wait(vec, dOp[par], x1_dma[tt - 2])
                t.wait(vec, av, ln2_apply[tt - 2])
            for b in range(4):
                t.inc(vec.tensor_tensor(s_x1t[:, par * C + b * 512: par * C + (b + 1) * 512],
                                        pbank(bset + b, 512),
                                        s_xstr[:, par * C + b * 512: par * C + (b + 1) * 512],
                                        ALU.add), dv)
            resid_ev[tt] = t.count[dv]
            t.wait(sync, dv, resid_ev[tt])
            for k in range(2):
                t.inc(sync.dma_start(out=d_x1[tt * P:(tt + 1) * P, k * KB:(k + 1) * KB],
                                     in_=s_x1t[:, par * C + k * KB: par * C + (k + 1) * KB]),
                      dOp[par], 16)
            x1_dma[tt] = t.count[dOp[par]]
            x1sl = s_x1t[:, par * C:(par + 1) * C]
            t.wait(vec, dv, resid_ev[tt])
            _, ln2_apply[tt] = emit_ln_tile(x1sl, par,
                                            ln2T_ev[tt - 2][0] if tt >= 2 else e_scan_pe)
            t.wait(ten, av, ln2_apply[tt])
            if tt == 0:
                t.wait(ten, dv, e_wg)
                t.wait(scal, dv, e_woutcv_dv)
                t.wait(scal, dVw[0], t.count["dVw0"])
                t.wait(scal, dVw[1], t.count["dVw1"])
            t.wait(ten, dv, resid_ev[tt])
            gbk = bset + 2
            for ch in range(NCH):
                bk = bset + ch % 2
                if tp2_ev:
                    t.wait(ten, av, tp2_ev)
                t.inc(ten.transpose(pbankb(bk, P),
                                    s_lntile[:, par * C + ch * P: par * C + (ch + 1) * P],
                                    s_ident[:]), pe)
                t.wait(scal, pe, t.count[pe])
                tp2_ev = t.inc(scal.copy(s_lnT[:, ch * KB + tt * P: ch * KB + (tt + 1) * P],
                                         pbankb(bk, P)), av)
                t.wait(ten, av, tp2_ev)
                if ch == 0 and lgt_ev:
                    t.wait(ten, av, lgt_ev)
                gmm = ten.matmul(pbank(gbk, NE), s_lnT[:, ch * KB + tt * P: ch * KB + (tt + 1) * P],
                                 s_wg[:, ch * NE:(ch + 1) * NE], start=(ch == 0), stop=(ch == NCH - 1))
            gpe = t.inc(gmm, pe)
            ln2T_ev[tt] = (t.count[pe], t.count[av])
            t.wait(scal, pe, gpe)
            if tt > 0:
                t.wait(scal, dv, g_ev[tt - 1])
            lgt_ev = t.inc(scal.copy(s_lgt[:], pbank(gbk, NE)), av)
            t.wait(vec, av, lgt_ev)
            t.inc(vec.max(s_max8[:], s_lgt[:]), dv)
            t.wait(scal, dv, t.count[dv])
            t.inc(scal.activation(s_dx1[:], s_max8[:, 0:1], AF.Exp, bias=s_max8[:, 1:2], scale=-1.0), av)
            t.wait(vec, av, t.count[av])
            t.inc(vec.tensor_scalar(s_w1[:], s_dx1[:], 1.0, None, ALU.add), dv)
            t.wait(vec, dv, t.count[dv])
            t.inc(vec.reciprocal(s_w1[:], s_w1[:]), dv)
            t.wait(vec, dv, t.count[dv])
            t.inc(vec.tensor_mul(s_w2[:], s_dx1[:], s_w1[:]), dv)
            t.wait(vec, dv, t.count[dv])
            t.inc(vec.tensor_scalar(s_g1[:], s_lgt[:], s_max8[:, 0:1], s_w1[:], ALU.is_equal, ALU.mult), dv)
            t.wait(vec, dv, t.count[dv])
            t.inc(vec.tensor_scalar(s_g2[:], s_lgt[:], s_max8[:, 1:2], s_w2[:], ALU.is_equal, ALU.mult), dv)
            t.wait(vec, dv, t.count[dv])
            t.inc(vec.tensor_add(s_g[:, tt * NE:(tt + 1) * NE], s_g1[:], s_g2[:]), dv)
            g_ev[tt] = t.count[dv]
    e_p7_pe = t.count[pe]
    e_p7_av = t.count[av]
    e_p7_dv = t.count[dv]

    seed_ev = {}
    for tt in range(NT):
        if tt == 0:
            t.wait(sync, pe, e_p7_pe)
            t.wait(sync, av, e_p7_av)
            t.wait(sync, dv, e_p7_dv)
            t.wait(sync, dOp[0], x1_dma[NT - 2])
            t.wait(sync, dOp[1], x1_dma[NT - 1])
            t.wait(sync, dDp[0], e_ffncv_st[0])
            t.wait(sync, dDp[1], e_ffncv_st[1])
        dma2(sync, s_out[:, tt * C:(tt + 1) * C], d_x1[tt * P:(tt + 1) * P, :], dS, C, 2)
        seed_ev[tt] = t.count[dS]

    wi_dma = {}
    f1_grp = {}
    silu_ev = {}
    inter_ev = {}
    wo_dma = {}
    y_evac = {}
    f2_mm = {}

    def emit_wi_load(e, hc):
        par = hc % 2
        idx = e * NHC + hc
        if idx >= 2:
            pidx = idx - 2
            t.wait(scal, pe, f1_grp[(pidx // NHC, pidx % NHC)])
        else:
            t.wait(scal, dv, e_p7_dv)
            t.wait(scal, av, e_p7_av)
            t.wait(scal, dDp[0], e_ffncv_st[0])
            t.wait(scal, dDp[1], e_ffncv_st[1])
        r0 = e * 3 * M + hc * P
        dma2(scal, s_wi[:, par * C:(par + 1) * C], d_wall[r0:r0 + P, :], dFp[par], C, 2)
        r1 = e * 3 * M + M + hc * P
        dma2(scal, s_wi[:, (2 + par) * C:(3 + par) * C], d_wall[r1:r1 + P, :], dFp[par], C, 2)
        wi_dma[(e, hc)] = (par, t.count[dFp[par]])

    def emit_wo_load(e, cs):
        key = e * 2 + cs
        par = key % 2
        if key >= 2:
            t.wait(sync, pe, f2_mm[(e - 1, NT - 1, cs)])
        else:
            t.wait(sync, pe, e_p7_pe)
            t.wait(sync, dDp[0], e_ffncv_st[0])
            t.wait(sync, dDp[1], e_ffncv_st[1])
        r0 = e * 3 * M + 2 * M
        for k in range(4):
            src = d_wall[r0 + k * 2 * P: r0 + (k + 1) * 2 * P,
                         cs * KB:(cs + 1) * KB].rearrange("(h p) n -> p h n", p=P)
            dst = s_wo[:, (par * NHC + 2 * k) * KB:(par * NHC + 2 * k + 2) * KB].rearrange(
                "p (h n) -> p h n", n=KB)
            t.inc(sync.dma_start(out=dst, in_=src), dGp[par], 16)
        wo_dma[(e, cs)] = (par, t.count[dGp[par]])

    emit_wi_load(0, 0)
    emit_wi_load(0, 1)
    emit_wo_load(0, 0)
    emit_wo_load(0, 1)

    for e in range(NE):
        for hc in range(NHC):
            par = hc % 2
            bset = 4 * par
            t.wait(ten, dFp[wi_dma[(e, hc)][0]], wi_dma[(e, hc)][1])
            if hc >= 2:
                t.wait(ten, av, silu_ev[(e, hc - 2)])
                t.wait(ten, dv, inter_ev[(e, hc - 2)])
            elif e >= 1:
                t.wait(ten, av, silu_ev[(e - 1, NHC - 2 + hc)])
                t.wait(ten, dv, inter_ev[(e - 1, NHC - 2 + hc)])
                t.wait(ten, dv, y_evac[(e - 1, NT - 1, 1)])
            else:
                t.wait(ten, av, e_p7_av)
                t.wait(ten, dv, e_p7_dv)
            wi0 = s_wi[:, par * C:(par + 1) * C]
            wi1 = s_wi[:, (2 + par) * C:(3 + par) * C]
            for ch in range(NCH):
                st_ = (ch == 0); sp = (ch == NCH - 1)
                lhs0 = wi0[:, ch * P:(ch + 1) * P]
                lhs1 = wi1[:, ch * P:(ch + 1) * P]
                rhs0 = s_lnT[:, ch * KB: ch * KB + 512]
                rhs1 = s_lnT[:, ch * KB + 512: (ch + 1) * KB]
                ten.matmul(pbank(bset + 0, 512), lhs0, rhs0, start=st_, stop=sp)
                ten.matmul(pbank(bset + 1, 512), lhs0, rhs1, start=st_, stop=sp)
                ten.matmul(pbank(bset + 2, 512), lhs1, rhs0, start=st_, stop=sp)
                mm = ten.matmul(pbank(bset + 3, 512), lhs1, rhs1, start=st_, stop=sp)
            f1_grp[(e, hc)] = t.inc(mm, pe)
            nxt_idx = e * NHC + hc + 2
            if nxt_idx < NE * NHC:
                emit_wi_load(nxt_idx // NHC, nxt_idx % NHC)
            t.wait(scal, pe, f1_grp[(e, hc)])
            if hc >= 2:
                t.wait(scal, dv, inter_ev[(e, hc - 2)])
            elif e >= 1:
                t.wait(scal, dv, inter_ev[(e - 1, NHC - 2 + hc)])
            t.inc(scal.activation(s_silu[:, par * KB: par * KB + 512], pbank(bset + 0, 512), AF.Silu), av)
            t.inc(scal.activation(s_silu[:, par * KB + 512:(par + 1) * KB], pbank(bset + 1, 512), AF.Silu), av)
            silu_ev[(e, hc)] = t.count[av]
            t.wait(vec, av, silu_ev[(e, hc)])
            if e >= 1 and hc == 0:
                t.wait(vec, pe, f2_mm[(e - 1, NT - 1, 1)])
            ib = hc * KB
            t.inc(vec.tensor_tensor(s_inter[:, ib:ib + 512],
                                    s_silu[:, par * KB: par * KB + 512], pbank(bset + 2, 512), ALU.mult), dv)
            t.inc(vec.tensor_tensor(s_inter[:, ib + 512:ib + KB],
                                    s_silu[:, par * KB + 512:(par + 1) * KB], pbank(bset + 3, 512),
                                    ALU.mult), dv)
            inter_ev[(e, hc)] = t.count[dv]
        ybanks = (2, 3)
        for cs in range(2):
            wpar = wo_dma[(e, cs)][0]
            wosl = s_wo[:, wpar * NHC * KB:(wpar + 1) * NHC * KB]
            t.wait(ten, dGp[wpar], wo_dma[(e, cs)][1])
            for tt in range(NT):
                if tt == 0 and cs == 0:
                    t.wait(ten, av, silu_ev[(e, NHC - 1)])
                    t.wait(ten, dv, inter_ev[(e, NHC - 1)])
                    t.wait(ten, dv, inter_ev[(e, NHC - 2)])
                if (e, tt - 1, cs) in y_evac:
                    t.wait(ten, dv, y_evac[(e, tt - 1, cs)])
                elif tt == 0 and cs == 1:
                    t.wait(ten, dv, y_evac[(e, NT - 1, 0)])
                for hc in range(NHC):
                    st_ = (hc == 0); sp = (hc == NHC - 1)
                    ib = hc * KB
                    lhs = s_inter[:, ib + tt * P: ib + (tt + 1) * P]
                    ten.matmul(pbank(ybanks[0], 512), lhs, wosl[:, hc * KB: hc * KB + 512],
                               start=st_, stop=sp)
                    mm = ten.matmul(pbank(ybanks[1], 512), lhs, wosl[:, hc * KB + 512:(hc + 1) * KB],
                                    start=st_, stop=sp)
                f2_mm[(e, tt, cs)] = t.inc(mm, pe)
                t.wait(vec, pe, f2_mm[(e, tt, cs)])
                if e == 0 and cs == 0:
                    t.wait(vec, dS, seed_ev[NT - 1])
                    t.wait(vec, dv, g_ev[tt])
                if e > 0:
                    t.wait(vec, dv, y_evac[(e - 1, tt, cs)])
                ob = tt * C + cs * KB
                gsc = s_g[:, tt * NE + e: tt * NE + e + 1]
                t.inc(vec.scalar_tensor_tensor(s_out[:, ob:ob + 512], pbank(ybanks[0], 512), gsc,
                                               s_out[:, ob:ob + 512], ALU.mult, ALU.add), dv)
                t.inc(vec.scalar_tensor_tensor(s_out[:, ob + 512:ob + KB], pbank(ybanks[1], 512), gsc,
                                               s_out[:, ob + 512:ob + KB], ALU.mult, ALU.add), dv)
                y_evac[(e, tt, cs)] = t.count[dv]
            if e + 1 < NE:
                emit_wo_load(e + 1, cs)

    for tt in range(NT):
        t.wait(sync, dv, y_evac[(NE - 1, tt, 1)])
        for k in range(2):
            t.inc(sync.dma_start(out=d_y[tt * P:(tt + 1) * P, k * KB:(k + 1) * KB],
                                 in_=s_out[:, tt * C + k * KB: tt * C + (k + 1) * KB]),
                  dOp[tt % 2], 16)

    if DEBUG_TAPS:
        t.wait(sync, av, t.count[av])
        t.wait(sync, dv, t.count[dv])
        t.inc(sync.dma_start(out=taps["tap_g"], in_=s_g[:]), dOp[0], 16)
        t.inc(sync.dma_start(out=taps["tap_x1"], in_=d_x1), dOp[0], 16)
        t.inc(sync.dma_start(out=taps["tap_S0"], in_=s_S0[:]), dOp[0], 16)

    t.wait(sync, dOp[0], t.count["dO0"])
    t.wait(sync, dOp[1], t.count["dO1"])

    stack.close()
    return nc


_PROG_CACHE = {}


def _get_program():
    key = ("prog", DEBUG_TAPS)
    if key not in _PROG_CACHE:
        _PROG_CACHE[key] = build_program()
    return _PROG_CACHE[key]


def make_in_maps(L0_attn_x, L0_attn_Wqkv, L0_attn_bqkv, L0_attn_Wout, L0_attn_bout,
                 L0_attn_slopes, L0_ffn_w_gate, L0_ffn_wi_0, L0_ffn_wi_1, L0_ffn_wo):
    f32 = np.float32
    x = np.ascontiguousarray(np.asarray(L0_attn_x, f32)).reshape(2, 4, KB, C)
    wqkv = np.ascontiguousarray(np.asarray(L0_attn_Wqkv, f32))
    bqkv = np.ascontiguousarray(np.asarray(L0_attn_bqkv, f32))
    wout = np.ascontiguousarray(np.asarray(L0_attn_Wout, f32))
    bout = np.ascontiguousarray(np.asarray(L0_attn_bout, f32))
    slopes = np.ascontiguousarray(np.asarray(L0_attn_slopes, f32))
    wg = np.asarray(L0_ffn_w_gate, f32)
    wi0 = np.asarray(L0_ffn_wi_0, f32)
    wi1 = np.asarray(L0_ffn_wi_1, f32)
    wo = np.asarray(L0_ffn_wo, f32)

    wg_im = np.ascontiguousarray(wg.reshape(NCH, P, NE).transpose(1, 0, 2).reshape(P, NCH * NE))
    ident = np.eye(P, dtype=np.float32).astype(ml_dtypes.bfloat16)
    jj = np.arange(P, dtype=f32)
    tri = np.ascontiguousarray((jj[None, :] >= jj[:, None]).astype(f32))
    cols = np.ascontiguousarray(
        np.stack([jj, P - jj, np.full(P, float(P), f32)], axis=1).astype(f32))
    tsc = (P - jj)[:, None] + P * (NT - 1 - np.arange(NT, dtype=f32))[None, :]
    tsc = np.ascontiguousarray(tsc.astype(f32))
    iota_row = jj[None, :]
    rows = np.concatenate([iota_row, np.zeros((1, P), f32)], axis=1).astype(f32)

    def wi_im(w):
        return np.ascontiguousarray(
            w.reshape(NCH, P, NHC, P).transpose(2, 1, 0, 3).reshape(M, C))

    blocks = []
    for e in range(NE):
        blocks.append(wi_im(wi0[e]))
        blocks.append(wi_im(wi1[e]))
        blocks.append(np.ascontiguousarray(wo[e]))
    wffn_im = np.ascontiguousarray(np.concatenate(blocks, axis=0))

    in_maps = []
    for r in range(8):
        b, s = r // 4, r % 4
        xp = np.zeros((NPT * P, C), f32)
        if s > 0:
            pref = x[b, 0:s].reshape(s * KB, C)
            xp[NPT * P - s * KB:] = pref
        nvalid = s * 8
        ptsc = np.full((P, NPT), 1e9, f32)
        for pt in range(NPT - nvalid, NPT):
            ptsc[:, pt] = (P - jj) + P * (NPT - 1 - pt)
        in_maps.append({
            "xs": np.ascontiguousarray(x[b, s]),
            "xp": xp,
            "wqkv": wqkv, "bqkv": bqkv, "wout": wout, "bout": bout,
            "slopes": slopes, "wg_im": wg_im, "wffn_im": wffn_im,
            "c_ident": ident, "c_tri": tri, "c_cols": cols, "c_tsc": tsc,
            "c_ptsc": np.ascontiguousarray(ptsc),
            "c_rows": rows,
        })
    return in_maps


def kernel(L0_attn_x, L0_attn_Wqkv, L0_attn_bqkv, L0_attn_Wout, L0_attn_bout,
           L0_attn_slopes, L0_ffn_w_gate, L0_ffn_wi_0, L0_ffn_wi_1, L0_ffn_wo):
    nc = _get_program()
    in_maps = make_in_maps(L0_attn_x, L0_attn_Wqkv, L0_attn_bqkv, L0_attn_Wout, L0_attn_bout,
                           L0_attn_slopes, L0_ffn_w_gate, L0_ffn_wi_0, L0_ffn_wi_1, L0_ffn_wo)
    res = run_bass_kernel_spmd(nc, in_maps, core_ids=list(range(8)))
    shards = [res.results[r]["y"] for r in range(8)]
    out = np.stack(shards, axis=0).reshape(2, 4 * KB, C).astype(np.float32)
    return out


# version 08
# speedup vs baseline: 1.0260x; 1.0260x over previous
import contextlib
import numpy as np
import ml_dtypes

import concourse.bass as bass
import concourse.mybir as mybir
from concourse.bass_utils import run_bass_kernel_spmd

F32 = mybir.dt.float32
BF16 = mybir.dt.bfloat16
AF = mybir.ActivationFunctionType
ALU = mybir.AluOpType

P = 128
NT = 8
NPT = 24
H = 16
C = 2048
NCH = 16
NE = 8
M = 1024
NHC = 8
KB = 1024
KiB = 1024

DEBUG_TAPS = False


class Tracker:
    def __init__(self, nc, stack):
        self.nc = nc
        self.count = {}
        self.handle = {}
        self.stack = stack

    def sem(self, name):
        self.handle[name] = self.stack.enter_context(self.nc.semaphore(name))
        self.count[name] = 0
        return name

    def inc(self, inst, s, n=1):
        self.count[s] += n
        inst.then_inc(self.handle[s], n)
        return self.count[s]

    def wait(self, eng, s, val=None):
        v = self.count[s] if val is None else val
        assert v <= self.count[s], f"wait on {s} for future value {v} > {self.count[s]}"
        if v > 0:
            eng.wait_ge(self.handle[s], v)


def build_program():
    nc = bass.Bass("TRN2", target_bir_lowering=False)
    stack = contextlib.ExitStack()
    t = Tracker(nc, stack)

    d_x = nc.dram_tensor("xs", [KB, C], F32, kind="ExternalInput").ap()
    d_xp = nc.dram_tensor("xp", [NPT * P, C], F32, kind="ExternalInput").ap()
    d_wqkv = nc.dram_tensor("wqkv", [C, 3 * C], F32, kind="ExternalInput").ap()
    d_bqkv = nc.dram_tensor("bqkv", [3 * C], F32, kind="ExternalInput").ap()
    d_wout = nc.dram_tensor("wout", [C, C], F32, kind="ExternalInput").ap()
    d_bout = nc.dram_tensor("bout", [C], F32, kind="ExternalInput").ap()
    d_slopes = nc.dram_tensor("slopes", [H], F32, kind="ExternalInput").ap()
    d_wg = nc.dram_tensor("wg_im", [P, NCH * NE], F32, kind="ExternalInput").ap()
    d_wffn = nc.dram_tensor("wffn_im", [NE * 3 * M, C], F32, kind="ExternalInput").ap()
    d_ident = nc.dram_tensor("c_ident", [P, P], BF16, kind="ExternalInput").ap()
    d_tri = nc.dram_tensor("c_tri", [P, P], F32, kind="ExternalInput").ap()
    d_cols = nc.dram_tensor("c_cols", [P, 3], F32, kind="ExternalInput").ap()
    d_tsc = nc.dram_tensor("c_tsc", [P, NT], F32, kind="ExternalInput").ap()
    d_ptsc = nc.dram_tensor("c_ptsc", [P, NPT], F32, kind="ExternalInput").ap()
    d_rows = nc.dram_tensor("c_rows", [1, 2 * P], F32, kind="ExternalInput").ap()
    d_y = nc.dram_tensor("y", [KB, C], F32, kind="ExternalOutput").ap()

    d_wqkvb = nc.dram_tensor("wqkvb", [C, 3 * C], BF16, kind="Internal").ap()
    d_wall = nc.dram_tensor("wall", [NE * 3 * M, C], BF16, kind="Internal").ap()
    d_woutb = nc.dram_tensor("woutb", [C, C], BF16, kind="Internal").ap()
    d_x1 = nc.dram_tensor("x1d", [KB, C], F32, kind="Internal").ap()
    d_biasb = nc.dram_tensor("biasb", [1, 4 * C], BF16, kind="Internal").ap()

    taps = {}
    if DEBUG_TAPS:
        taps["tap_g"] = nc.dram_tensor("tap_g", [P, NT * NE], F32, kind="ExternalOutput").ap()
        taps["tap_x1"] = nc.dram_tensor("tap_x1", [KB, C], F32, kind="ExternalOutput").ap()
        taps["tap_S0"] = nc.dram_tensor("tap_S0", [P, H * P], F32, kind="ExternalOutput").ap()

    B = 18 * KiB

    def at(name, shape, dtype, off):
        assert off % 32 == 0
        return nc.alloc_sbuf_tensor_at(name, shape, dtype, offset=B + off)

    o = 0

    def nxt(name, shape, dtype):
        nonlocal o
        h = at(name, shape, dtype, o)
        sz = int(np.prod(shape[1:])) * (2 if dtype == BF16 else 4)
        o += (sz + 31) // 32 * 32
        return h

    s_decay = nxt("s_decay", [P, H * P], F32)
    s_bout = nxt("s_bout", [1, 2 * KB], BF16)
    s_qdec = nxt("s_qdec", [P, H], F32)
    s_kdec = nxt("s_kdec", [P, H], F32)
    s_dcoef = nxt("s_dcoef", [P, H], F32)
    s_negs = nxt("s_negs", [P, H], F32)
    s_tri = nxt("s_tri", [P, P], F32)
    s_cols = nxt("s_cols", [P, 4], F32)
    s_tsc = nxt("s_tsc", [P, NT], F32)
    s_ptsc = nxt("s_ptsc", [P, NPT], F32)
    s_rs = nxt("s_rs", [P, NT], F32)
    s_ident = nxt("s_ident", [P, P], BF16)
    s_rows32 = nxt("s_rows32", [1, 2 * P], F32)
    s_srow = nxt("s_srow", [1, P], F32)
    s_nsrow = nxt("s_nsrow", [1, P], F32)
    s_wg = nxt("s_wg", [P, NCH * NE], BF16)
    s_Sbf = nxt("s_Sbf", [P, 2 * P], BF16)
    s_qT = nxt("s_qT", [P, 2 * P], BF16)
    s_qdT = nxt("s_qdT", [P, 2 * P], BF16)
    s_kT = nxt("s_kT", [P, 2 * P], BF16)
    s_scm = nxt("s_scm", [P, 2 * P], BF16)
    s_kbd = nxt("s_kbd", [P, 2 * P], BF16)
    s_qbd = nxt("s_qbd", [P, 2 * P], BF16)
    s_scr32 = at("s_scr32", [P, KB], F32, 85 * KiB)
    s_silu = at("s_silu", [P, 2 * KB], BF16, 173 * KiB)
    s_dmm = s_scr32.ap()[:, 0:P]
    s_wgf = s_scr32.ap()[:, P:2 * P]
    s_stat = nxt("s_stat", [P, 48], F32)
    s_aggr = nxt("s_aggr", [P, 4], F32)
    s_rstd = nxt("s_rstd", [P, 2], F32)
    s_nmr = nxt("s_nmr", [P, 2], F32)
    s_eps = nxt("s_eps", [P, 1], F32)
    s_lgt = nxt("s_lgt", [P, NE], F32)
    s_max8 = nxt("s_max8", [P, NE], F32)
    s_dx1 = nxt("s_dx1", [P, 1], F32)
    s_w1 = nxt("s_w1", [P, 1], F32)
    s_w2 = nxt("s_w2", [P, 1], F32)
    s_g1 = nxt("s_g1", [P, NE], F32)
    s_g2 = nxt("s_g2", [P, NE], F32)
    s_g = nxt("s_g", [P, NT * NE], F32)
    s_onesb = nxt("s_onesb", [1, P], BF16)
    s_onesf = nxt("s_onesf", [1, P], F32)
    s_S0 = nxt("s_S0", [P, H * P], F32)
    assert o <= 29 * KiB, o

    s_lnT = at("s_lnT", [P, NCH * KB], BF16, 29 * KiB)
    s_q = at("s_q", [P, NT * C], BF16, 61 * KiB)
    s_k = at("s_k", [P, NT * C], BF16, 93 * KiB)
    s_v = at("s_v", [P, NT * C], BF16, 125 * KiB)
    s_cvf = at("s_cvf", [P, 2 * C], F32, 61 * KiB)
    s_cvb = at("s_cvb", [P, 2 * C], BF16, 77 * KiB)
    s_wqcf = at("s_wqcf", [P, 2 * 3 * C], F32, 61 * KiB)
    s_wqcb = at("s_wqcb", [P, 2 * 3 * C], BF16, 125 * KiB)
    s_bst = at("s_bst", [1, C], F32, 109 * KiB)
    s_rowsbf = at("s_rowsbf", [1, C], BF16, 117 * KiB)
    s_kvst = at("s_kvst", [P, 4 * 4096], BF16, 93 * KiB)
    s_lntile = at("s_lntile", [P, 2 * C], BF16, 85 * KiB)
    s_x1t = at("s_x1t", [P, 2 * C], F32, 93 * KiB)
    s_wostr = at("s_wostr", [P, 2 * C], BF16, 109 * KiB)
    s_out = at("s_out", [P, NT * C], F32, 61 * KiB)
    s_wo = at("s_wo", [P, 2 * NHC * KB], BF16, 125 * KiB)
    s_wcg = at("s_wcg", [P, NCH * KB], BF16, 157 * KiB)
    s_aT = at("s_aT", [P, H * KB], BF16, 157 * KiB)
    s_inter = at("s_inter", [P, NHC * KB], BF16, 157 * KiB)
    s_cvf2 = at("s_cvf2", [P, 2 * C], F32, 29 * KiB)
    s_cvb2 = at("s_cvb2", [P, 2 * C], BF16, 45 * KiB)
    s_xstr = at("s_xstr", [P, 2 * C], F32, 189 * KiB)
    s_wi = at("s_wi", [P, 4 * C], BF16, 189 * KiB)

    pbs = [stack.enter_context(nc.psum_tensor(f"pb{i}", [P, 512], F32)) for i in range(8)]

    def pbank(i, n=512, off=0):
        return pbs[i].ap()[:, off:off + n]

    def pbankb(i, n=1024, off=0):
        return pbs[i].ap().bitcast(BF16)[:, off:off + n]

    pe = t.sem("pe"); av = t.sem("av"); dv = t.sem("dv")
    dC = t.sem("dC")
    dXp = [t.sem("dX0"), t.sem("dX1")]
    dWp = [t.sem("dW0"), t.sem("dW1")]
    dQp = [t.sem("dQ0"), t.sem("dQ1")]
    dQs = [t.sem("dQs0"), t.sem("dQs1")]
    dEp = [t.sem("dE0"), t.sem("dE1")]
    dDp = [t.sem("dD0"), t.sem("dD1")]
    dVw = [t.sem("dVw0"), t.sem("dVw1")]
    dFp = [t.sem("dF0"), t.sem("dF1")]
    dGp = [t.sem("dG0"), t.sem("dG1")]
    dOp = [t.sem("dO0"), t.sem("dO1")]
    dS = t.sem("dS")
    dB = t.sem("dB")
    dR = t.sem("dR")

    sync, scal, vec, ten = nc.sync, nc.scalar, nc.vector, nc.tensor

    t.inc(sync.dma_start(out=s_ident[:], in_=d_ident), dC, 16)
    t.inc(sync.dma_start(out=s_tri[:], in_=d_tri), dC, 16)
    t.inc(sync.dma_start(out=s_cols[:, 0:3], in_=d_cols), dC, 16)
    t.inc(sync.dma_start(out=s_tsc[:], in_=d_tsc), dC, 16)
    t.inc(sync.dma_start(out=s_ptsc[:], in_=d_ptsc), dC, 16)
    t.inc(sync.dma_start(out=s_rows32[:], in_=d_rows), dC, 16)
    t.inc(sync.dma_start(out=s_wgf, in_=d_wg), dC, 16)
    for r in range(8):
        t.inc(sync.dma_start(out=s_srow[0:1, r * H:(r + 1) * H], in_=d_slopes[None, :]), dC, 16)
    e_dc = t.count[dC]
    bias_cvt = 0
    bias_stv = 0
    for bc in range(4):
        if bc > 0:
            t.wait(sync, av, bias_cvt)
        if bc < 3:
            t.inc(sync.dma_start(out=s_bst[0:1, :], in_=d_bqkv[None, bc * C:(bc + 1) * C]), dB, 16)
        else:
            t.inc(sync.dma_start(out=s_bst[0:1, :], in_=d_bout[None, :]), dB, 16)
        t.wait(scal, dB, t.count[dB])
        t.inc(scal.copy(s_rowsbf[0:1, :], s_bst[0:1, :]), av)
        bias_cvt = t.count[av]
        t.wait(sync, av, bias_cvt)
        t.inc(sync.dma_start(out=d_biasb[0:1, bc * C:(bc + 1) * C], in_=s_rowsbf[0:1, :]), dB, 16)
        bias_stv = t.count[dB]
    e_bias_bf = bias_cvt
    e_bias_st = bias_stv
    t.wait(vec, dC, e_dc)
    t.inc(vec.tensor_copy(s_wg[:], s_wgf), dv)
    e_wg = t.count[dv]
    t.inc(vec.memset(s_eps[:], 1e-5), dv)
    t.inc(vec.memset(s_onesb[:], 1.0), dv)
    t.inc(vec.memset(s_onesf[:], 1.0), dv)
    t.inc(vec.tensor_scalar(s_nsrow[0:1, 0:H], s_srow[0:1, 0:H], -1.0, None, ALU.mult), dv)
    e_setup_dv = t.count[dv]

    t.wait(ten, dv, e_setup_dv)
    ten.matmul(pbank(0, H), s_onesf[:], s_nsrow[0:1, 0:H], start=True, stop=True)
    e_bc = t.inc(ten.matmul(pbank(2, P), s_onesf[:], s_rows32[0:1, 0:P], start=True, stop=True), pe)
    t.wait(vec, pe, e_bc)
    t.inc(vec.tensor_copy(s_negs[:], pbank(0, H)), dv)
    t.inc(vec.tensor_scalar(s_dmm, pbank(2, P), s_cols[:, 0:1], None, ALU.subtract), dv)
    t.wait(vec, dv, t.count[dv])
    t.inc(vec.tensor_mul(s_dmm, s_dmm, s_tri[:]), dv)
    e_dmm = t.count[dv]

    t.wait(scal, dv, e_dmm)
    for h in range(H):
        t.inc(scal.activation(s_decay[:, h * P:(h + 1) * P], s_dmm, AF.Exp,
                              scale=s_negs[:, h:h + 1]), av)
        t.inc(scal.activation(s_qdec[:, h:h + 1], s_cols[:, 0:1], AF.Exp,
                              scale=s_negs[:, h:h + 1]), av)
        t.inc(scal.activation(s_kdec[:, h:h + 1], s_cols[:, 1:2], AF.Exp,
                              scale=s_negs[:, h:h + 1]), av)
        t.inc(scal.activation(s_dcoef[:, h:h + 1], s_cols[:, 2:3], AF.Exp,
                              scale=s_negs[:, h:h + 1]), av)
    e_decay_exp = t.count[av]
    t.wait(vec, av, e_decay_exp)
    for h in range(H):
        t.inc(vec.tensor_mul(s_decay[:, h * P:(h + 1) * P],
                             s_decay[:, h * P:(h + 1) * P], s_tri[:]), dv)
    e_decay = t.count[dv]

    wq_cvt = {}
    wq_st = {}
    for i in range(NCH):
        par = i % 2
        if i >= 2:
            t.wait(sync, dv, wq_cvt[i - 2])
        t.inc(sync.dma_start(out=s_wqcf[:, par * 3 * C:(par + 1) * 3 * C],
                             in_=d_wqkv[i * P:(i + 1) * P, :]), dQp[par], 16)
        t.wait(vec, dQp[par], t.count[dQp[par]])
        if i >= 2:
            t.wait(vec, dQs[par], wq_st[i - 2])
        t.inc(vec.tensor_copy(s_wqcb[:, par * 3 * C:(par + 1) * 3 * C],
                              s_wqcf[:, par * 3 * C:(par + 1) * 3 * C]), dv)
        wq_cvt[i] = t.count[dv]
        t.wait(sync, dv, wq_cvt[i])
        t.inc(sync.dma_start(out=d_wqkvb[i * P:(i + 1) * P, :],
                             in_=s_wqcb[:, par * 3 * C:(par + 1) * 3 * C]), dQs[par], 16)
        wq_st[i] = t.count[dQs[par]]
    e_wq_pe_gate = t.count[dv]

    ffn_cv = {"i": 0, "rd": {}, "st": {}}

    def emit_ffn_conv(n):
        for _ in range(n):
            i = ffn_cv["i"]
            if i >= NE * 3 * NHC:
                return
            ffn_cv["i"] = i + 1
            par = i % 2
            if i >= 2:
                t.wait(scal, dv, ffn_cv["rd"][i - 2])
            elif i == 0:
                t.wait(scal, dv, e_wq_pe_gate)
            t.inc(scal.dma_start(out=s_cvf[:, par * C:(par + 1) * C],
                                 in_=d_wffn[i * P:(i + 1) * P, :]), dEp[par], 16)
            t.wait(vec, dEp[par], t.count[dEp[par]])
            if i >= 2:
                t.wait(vec, dDp[par], ffn_cv["st"][i - 2])
            t.inc(vec.tensor_copy(s_cvb[:, par * C:(par + 1) * C],
                                  s_cvf[:, par * C:(par + 1) * C]), dv)
            ffn_cv["rd"][i] = t.count[dv]
            t.wait(scal, dv, ffn_cv["rd"][i])
            t.inc(scal.dma_start(out=d_wall[i * P:(i + 1) * P, :],
                                 in_=s_cvb[:, par * C:(par + 1) * C]), dDp[par], 16)
            ffn_cv["st"][i] = t.count[dDp[par]]

    emit_ffn_conv(24)

    ln_state = {"stats_dv": {}, "apply": {}, "tp": 0}

    def emit_ln_tile(xsl, par, prev_pe_gate):
        stt = s_stat[:, par * 24:par * 24 + 24]
        agg = s_aggr[:, par * 2:par * 2 + 2]
        rst = s_rstd[:, par:par + 1]
        nmr = s_nmr[:, par:par + 1]
        for g in range(4):
            t.inc(vec.bn_stats(stt[:, 6 * g:6 * g + 6], xsl[:, 512 * g:512 * (g + 1)]), dv)
        t.wait(vec, dv, t.count[dv])
        t.inc(vec.bn_aggr(agg, stt), dv)
        e_ag = t.count[dv]
        t.wait(scal, dv, e_ag)
        t.inc(scal.activation(rst, agg[:, 1:2], AF.Sqrt, bias=s_eps[:]), av)
        t.wait(vec, av, t.count[av])
        t.inc(vec.reciprocal(rst, rst), dv)
        t.wait(vec, dv, t.count[dv])
        t.inc(vec.tensor_scalar(nmr, agg[:, 0:1], rst, -1.0, ALU.mult, ALU.mult), dv)
        e_nmr = t.count[dv]
        t.wait(scal, dv, e_nmr)
        if prev_pe_gate:
            t.wait(scal, pe, prev_pe_gate)
        t.inc(scal.activation(s_lntile[:, par * C:(par + 1) * C], xsl, AF.Identity,
                              bias=nmr, scale=rst), av)
        return t.count[dv], t.count[av]

    def emit_transposes(par, slot, banks=(0, 1)):
        t.wait(ten, av, t.count[av])
        for ch in range(NCH):
            bk = banks[ch % 2]
            if ln_state["tp"]:
                t.wait(ten, av, ln_state["tp"])
            t.inc(ten.transpose(pbankb(bk, P), s_lntile[:, par * C + ch * P: par * C + (ch + 1) * P],
                                s_ident[:]), pe)
            t.wait(scal, pe, t.count[pe])
            ln_state["tp"] = t.inc(scal.copy(s_lnT[:, ch * KB + slot * P: ch * KB + (slot + 1) * P],
                                             pbankb(bk, P)), av)
        return t.count[pe], t.count[av]

    t.wait(vec, dv, e_decay)
    t.inc(vec.memset(s_S0[:], 0.0), dv)
    e_s0_ms = t.count[dv]
    wcg_user = {}
    wcg_ld = {}
    ub_dv = {}
    e_prev_u_dv = 0
    brow_ld = 0
    x_ev = {}
    pfx_lnT_pe = 0
    pfx_u_mm = {}
    e_prev_kv_mm = 0
    e_prefix_pe = 0
    sbf_gate = 0
    for g in range(6):
        for pt4 in range(4):
            pt = g * 4 + pt4
            par = pt % 2
            if pt >= 2:
                t.wait(sync, av, ln_state["apply"].get(pt - 2, 0))
                t.wait(sync, dv, ln_state["stats_dv"].get(pt - 2, 0))
            t.inc(sync.dma_start(out=s_xstr[:, par * C:(par + 1) * C],
                                 in_=d_xp[pt * P:(pt + 1) * P, :]), dXp[par], 16)
            x_ev[pt] = t.count[dXp[par]]
            t.wait(vec, dXp[par], x_ev[pt])
            if pt == 0:
                t.wait(vec, dv, e_s0_ms)
            sd, ap = emit_ln_tile(s_xstr[:, par * C:(par + 1) * C], par,
                                  pfx_lnT_pe if pt >= 2 else 0)
            ln_state["stats_dv"][pt] = sd
            ln_state["apply"][pt] = ap
            pfx_lnT_pe, _ = emit_transposes(par, pt4)
        emit_ffn_conv(28)
        for cg in range(4):
            colbase = C + cg * KB
            t.wait(sync, dB, e_bias_st)
            if brow_ld:
                t.wait(sync, pe, brow_pe_gate)
            t.inc(sync.dma_start(out=s_bout[0:1, 0:KB], in_=d_biasb[0:1, colbase:colbase + KB]), dR, 16)
            brow_ld = t.count[dR]
            for hf in range(2):
                if (g, cg) != (0, 0):
                    t.wait(sync, pe, wcg_user.get(0, 0))
                elif hf == 0:
                    t.wait(sync, dQs[0], wq_st[NCH - 2])
                    t.wait(sync, dQs[1], wq_st[NCH - 1])
                src = d_wqkvb[hf * KB:(hf + 1) * KB, colbase:colbase + KB].rearrange(
                    "(c p) n -> p c n", p=P)
                dst = s_wcg[:, hf * 8 * KB:(hf + 1) * 8 * KB].rearrange("p (c n) -> p c n", n=KB)
                t.inc(sync.dma_start(out=dst, in_=src), dWp[hf], 16)
                wcg_ld[hf] = t.count[dWp[hf]]
            t.wait(ten, av, t.count[av])
            if e_prev_kv_mm:
                t.wait(ten, av, kv_evac_ev)
            if e_prev_u_dv:
                t.wait(ten, dv, e_prev_u_dv)
            t.wait(ten, dWp[0], wcg_ld[0])
            t.wait(ten, dWp[1], wcg_ld[1])
            for i in range(NCH):
                for pt4 in range(4):
                    lhs = s_lnT[:, i * KB + pt4 * P: i * KB + (pt4 + 1) * P]
                    st_ = (i == 0)
                    ten.matmul(pbank(2 * pt4, 512), lhs, s_wcg[:, i * KB:i * KB + 512],
                               start=st_, stop=False)
                    ten.matmul(pbank(2 * pt4 + 1, 512), lhs, s_wcg[:, i * KB + 512:(i + 1) * KB],
                               start=st_, stop=False)
            t.wait(ten, dR, brow_ld)
            for pt4 in range(4):
                ten.matmul(pbank(2 * pt4, 512), s_onesb[:], s_bout[0:1, 0:512],
                           start=False, stop=True)
                mm = ten.matmul(pbank(2 * pt4 + 1, 512), s_onesb[:], s_bout[0:1, 512:KB],
                                start=False, stop=True)
            e_prev_kv_mm = t.inc(mm, pe)
            brow_pe_gate = e_prev_kv_mm
            wcg_user[0] = e_prev_kv_mm
            t.wait(scal, pe, e_prev_kv_mm)
            if g == 0 and cg == 0:
                t.wait(scal, dv, e_wg)
            for pt4 in range(4):
                base = pt4 * 4096 + cg * KB
                t.inc(scal.copy(s_kvst[:, base:base + 512], pbank(2 * pt4, 512)), av)
                t.inc(scal.copy(s_kvst[:, base + 512:base + KB], pbank(2 * pt4 + 1, 512)), av)
            kv_evac_ev = t.count[av]
        t.wait(ten, av, kv_evac_ev)
        for h in range(H):
            par2 = h % 2
            t.wait(scal, av, t.count[av])
            t.inc(scal.activation(s_rs[:, 0:4], s_ptsc[:, g * 4:(g + 1) * 4], AF.Exp,
                                  scale=s_negs[:, h:h + 1]), av)
            t.wait(scal, av, t.count[av])
            if pfx_u_mm.get(par2):
                t.wait(scal, pe, pfx_u_mm[par2])
            for pt4 in range(4):
                dstsl = (s_kbd, s_qbd, s_qT, s_qdT)[pt4][:, par2 * P:(par2 + 1) * P]
                t.inc(scal.activation(dstsl,
                                      s_kvst[:, pt4 * 4096 + h * P: pt4 * 4096 + (h + 1) * P],
                                      AF.Identity, scale=s_rs[:, pt4:pt4 + 1]), av)
            e_ksc = t.count[av]
            t.wait(ten, av, e_ksc)
            if par2 in ub_dv:
                t.wait(ten, dv, ub_dv[par2])
            for pt4 in range(4):
                srcsl = (s_kbd, s_qbd, s_qT, s_qdT)[pt4][:, par2 * P:(par2 + 1) * P]
                mm = ten.matmul(pbank(4 + par2, P), srcsl,
                                s_kvst[:, pt4 * 4096 + 2048 + h * P: pt4 * 4096 + 2048 + (h + 1) * P],
                                start=(pt4 == 0), stop=(pt4 == 3))
            pfx_u_mm[par2] = t.inc(mm, pe)
            t.wait(vec, pe, pfx_u_mm[par2])
            t.wait(vec, dv, t.count[dv])
            t.inc(vec.tensor_tensor(s_S0[:, h * P:(h + 1) * P], s_S0[:, h * P:(h + 1) * P],
                                    pbank(4 + par2, P), ALU.add), dv)
            ub_dv[par2] = t.count[dv]
        e_prev_u_dv = t.count[dv]
        e_prefix_pe = t.count[pe]
        emit_ffn_conv(2)
    e_S0 = t.count[dv]
    e_prefix_av = t.count[av]

    ln_state["tp"] = ln_state["tp"]
    lnT_ev = {}
    own_apply = {}
    for tt in range(NT):
        par = tt % 2
        if tt < 2:
            t.wait(sync, av, ln_state["apply"][NPT - 2 + tt])
            t.wait(sync, dv, ln_state["stats_dv"][NPT - 2 + tt])
        else:
            t.wait(sync, av, own_apply[tt - 2])
            t.wait(sync, dv, own_stats[tt - 2])
        t.inc(sync.dma_start(out=s_xstr[:, par * C:(par + 1) * C],
                             in_=d_x[tt * P:(tt + 1) * P, :]), dXp[par], 16)
        xev = t.count[dXp[par]]
        t.wait(vec, dXp[par], xev)
        sd, ap = emit_ln_tile(s_xstr[:, par * C:(par + 1) * C], par,
                              lnT_ev[tt - 2][0] if tt >= 2 else e_prefix_pe)
        if tt == 0:
            own_stats = {}
        own_stats[tt] = sd
        own_apply[tt] = ap
        lnT_ev[tt] = emit_transposes(par, tt)
    e_ln1_av = t.count[av]
    e_ln1_dv = t.count[dv]

    dest = [(s_q, 0), (s_q, 1), (s_k, 0), (s_k, 1), (s_v, 0), (s_v, 1)]
    qkv_mm_done = {}
    qkv_evac = {}
    for cg in range(6):
        t.wait(sync, dB, e_bias_st)
        t.wait(sync, pe, brow_pe_gate)
        t.inc(sync.dma_start(out=s_bout[0:1, 0:KB], in_=d_biasb[0:1, cg * KB:(cg + 1) * KB]), dR, 16)
        brow_ld = t.count[dR]
        for hf in range(2):
            t.wait(sync, pe, wcg_user[0])
            src = d_wqkvb[hf * KB:(hf + 1) * KB, cg * KB:(cg + 1) * KB].rearrange(
                "(c p) n -> p c n", p=P)
            dst = s_wcg[:, hf * 8 * KB:(hf + 1) * 8 * KB].rearrange("p (c n) -> p c n", n=KB)
            t.inc(sync.dma_start(out=dst, in_=src), dWp[hf], 16)
            wcg_ld[hf] = t.count[dWp[hf]]
        for grp in range(2):
            tts = list(range(grp * 4, grp * 4 + 4))
            t.wait(ten, av, e_ln1_av if cg == 0 else lnT_ev[max(tts)][1])
            if grp == 1:
                t.wait(ten, av, qkv_evac[(cg, 0)])
            elif cg > 0:
                t.wait(ten, av, qkv_evac[(cg - 1, 1)])
            else:
                t.wait(ten, av, e_prefix_av)
                t.wait(ten, dv, e_S0)
            t.wait(ten, dWp[0], wcg_ld[0])
            t.wait(ten, dWp[1], wcg_ld[1])
            for i in range(NCH):
                for j, tt in enumerate(tts):
                    lhs = s_lnT[:, i * KB + tt * P: i * KB + (tt + 1) * P]
                    st_ = (i == 0)
                    ten.matmul(pbank(2 * j, 512), lhs, s_wcg[:, i * KB:i * KB + 512],
                               start=st_, stop=False)
                    ten.matmul(pbank(2 * j + 1, 512), lhs, s_wcg[:, i * KB + 512:(i + 1) * KB],
                               start=st_, stop=False)
            t.wait(ten, dR, brow_ld)
            for j, tt in enumerate(tts):
                ten.matmul(pbank(2 * j, 512), s_onesb[:], s_bout[0:1, 0:512],
                           start=False, stop=True)
                mm = ten.matmul(pbank(2 * j + 1, 512), s_onesb[:], s_bout[0:1, 512:KB],
                                start=False, stop=True)
            qkv_mm_done[(cg, grp)] = t.inc(mm, pe)
            if grp == 1:
                brow_pe_gate = qkv_mm_done[(cg, grp)]
                wcg_user[0] = qkv_mm_done[(cg, grp)]
            t.wait(scal, pe, qkv_mm_done[(cg, grp)])
            if cg == 0 and grp == 0:
                t.wait(scal, dEp[0], t.count["dE0"])
                t.wait(scal, dEp[1], t.count["dE1"])
                t.wait(scal, dDp[0], t.count["dD0"])
                t.wait(scal, dDp[1], t.count["dD1"])
                t.wait(scal, dQs[0], t.count["dQs0"])
                t.wait(scal, dQs[1], t.count["dQs1"])
                t.wait(scal, dVw[0], t.count["dVw0"])
                t.wait(scal, dVw[1], t.count["dVw1"])
            dst, half = dest[cg]
            for j, tt in enumerate(tts):
                base = tt * C + half * KB
                t.inc(scal.copy(dst[:, base:base + 512], pbank(2 * j, 512)), av)
                t.inc(scal.copy(dst[:, base + 512:base + KB], pbank(2 * j + 1, 512)), av)
            qkv_evac[(cg, grp)] = t.count[av]
        emit_ffn_conv(4)
    e_qkv_av = t.count[av]
    e_qkv_pe = t.count[pe]

    wo_st = {}
    wo_cvt = {}
    woc = {"i": 0}

    def emit_wout_conv(n):
        for _ in range(n):
            i = woc["i"]
            if i >= NCH:
                return
            woc["i"] = i + 1
            par = i % 2
            if i >= 2:
                t.wait(sync, dv, wo_cvt[i - 2])
            else:
                t.wait(sync, pe, qkv_mm_done[(5, 1)])
            t.inc(sync.dma_start(out=s_cvf2[:, par * C:(par + 1) * C],
                                 in_=d_wout[i * P:(i + 1) * P, :]), dQp[par], 16)
            t.wait(vec, dQp[par], t.count[dQp[par]])
            if i >= 2:
                t.wait(vec, dVw[par], wo_st[i - 2])
            t.inc(vec.tensor_copy(s_cvb2[:, par * C:(par + 1) * C],
                                  s_cvf2[:, par * C:(par + 1) * C]), dv)
            wo_cvt[i] = t.count[dv]
            t.wait(sync, dv, wo_cvt[i])
            t.inc(sync.dma_start(out=d_woutb[i * P:(i + 1) * P, :],
                                 in_=s_cvb2[:, par * C:(par + 1) * C]), dVw[par], 16)
            wo_st[i] = t.count[dVw[par]]

    last_scm = {0: 0, 1: 0}
    last_oT_evac = {0: 0, 1: 0}
    last_supd_add = {0: 0, 1: 0}
    qbd_tp_mm = {0: 0, 1: 0}
    supd_mm = {0: 0, 1: 0}
    oT_mm_ev = {0: 0, 1: 0}
    sbf_ev = {0: 0, 1: 0}
    last_tp_evac = {4: 0, 5: 0}
    SC = {0: 0, 1: 2}
    OT = {0: 1, 1: 7}
    SU = {0: 3, 1: 6}
    tpbank = [0]
    for hp in range(H // 2):
        for j in (0, 1):
            h = 2 * hp + j
            t.wait(vec, dv, t.count[dv])
            if oT_mm_ev[j]:
                t.wait(vec, pe, oT_mm_ev[j])
            elif hp == 0:
                t.wait(vec, dv, e_S0)
            t.inc(vec.tensor_copy(s_Sbf[:, j * P:(j + 1) * P], s_S0[:, h * P:(h + 1) * P]), dv)
            sbf_ev[j] = t.count[dv]
        for c in range(NT):
            for j in (0, 1):
                h = 2 * hp + j
                qt = s_q[:, c * C + h * P: c * C + (h + 1) * P]
                kt = s_k[:, c * C + h * P: c * C + (h + 1) * P]
                vt = s_v[:, c * C + h * P: c * C + (h + 1) * P]
                if hp == 0 and c == 0 and j == 0:
                    t.wait(scal, av, e_qkv_av)
                    t.wait(scal, pe, e_qkv_pe)
                if qbd_tp_mm[j]:
                    t.wait(scal, pe, qbd_tp_mm[j])
                t.inc(scal.activation(s_qbd[:, j * P:(j + 1) * P], qt, AF.Identity,
                                      scale=s_qdec[:, h:h + 1]), av)
                qbd_ev = t.count[av]
                if supd_mm[j]:
                    t.wait(scal, pe, supd_mm[j])
                t.inc(scal.activation(s_kbd[:, j * P:(j + 1) * P], kt, AF.Identity,
                                      scale=s_kdec[:, h:h + 1]), av)
                kbd_ev = t.count[av]
                if hp == 0 and c == 0 and j == 0:
                    t.wait(ten, av, e_qkv_av)
                    t.wait(ten, dv, e_S0)
                for which, src2, dst_sb, dep_av in (
                        (0, qt, s_qT, 0), (1, None, s_qdT, qbd_ev), (2, kt, s_kT, 0)):
                    bk = 4 + tpbank[0] % 2
                    tpbank[0] += 1
                    if last_tp_evac[bk]:
                        t.wait(ten, av, last_tp_evac[bk])
                    if dep_av:
                        t.wait(ten, av, dep_av)
                        src2 = s_qbd[:, j * P:(j + 1) * P]
                    mmev = t.inc(ten.transpose(pbankb(bk, P), src2, s_ident[:]), pe)
                    if which == 1:
                        qbd_tp_mm[j] = mmev
                    t.wait(scal, pe, mmev)
                    last_tp_evac[bk] = t.inc(scal.copy(dst_sb[:, j * P:(j + 1) * P],
                                                       pbankb(bk, P)), av)
                e_tp_av = t.count[av]
                t.wait(ten, av, e_tp_av)
                if last_scm[j]:
                    t.wait(ten, dv, last_scm[j])
                sc_mm = t.inc(ten.matmul(pbank(SC[j], P), s_kT[:, j * P:(j + 1) * P],
                                         s_qT[:, j * P:(j + 1) * P], start=True, stop=True), pe)
                t.wait(vec, pe, sc_mm)
                t.inc(vec.tensor_tensor(s_scm[:, j * P:(j + 1) * P], pbank(SC[j], P),
                                        s_decay[:, h * P:(h + 1) * P], ALU.mult), dv)
                last_scm[j] = t.count[dv]
                t.wait(ten, dv, last_scm[j])
                if c == 0:
                    t.wait(ten, dv, sbf_ev[j])
                if last_oT_evac[j]:
                    t.wait(ten, av, last_oT_evac[j])
                ten.matmul(pbank(OT[j], P), vt, s_scm[:, j * P:(j + 1) * P], start=True, stop=False)
                oT_mm_ev[j] = t.inc(ten.matmul(pbank(OT[j], P), s_Sbf[:, j * P:(j + 1) * P],
                                               s_qdT[:, j * P:(j + 1) * P],
                                               start=False, stop=True), pe)
                t.wait(scal, pe, oT_mm_ev[j])
                last_oT_evac[j] = t.inc(scal.copy(s_aT[:, h * KB + c * P: h * KB + (c + 1) * P],
                                                  pbank(OT[j], P)), av)
                if c < NT - 1:
                    t.wait(ten, av, kbd_ev)
                    if last_supd_add[j]:
                        t.wait(ten, dv, last_supd_add[j])
                    supd_mm[j] = t.inc(ten.matmul(pbank(SU[j], P), s_kbd[:, j * P:(j + 1) * P], vt,
                                                  start=True, stop=True), pe)
                    t.wait(vec, pe, oT_mm_ev[j])
                    t.wait(vec, dv, t.count[dv])
                    t.inc(vec.tensor_scalar(s_S0[:, h * P:(h + 1) * P], s_S0[:, h * P:(h + 1) * P],
                                            s_dcoef[:, h:h + 1], None, ALU.mult), dv)
                    t.wait(vec, pe, supd_mm[j])
                    t.wait(vec, dv, t.count[dv])
                    t.inc(vec.tensor_tensor(s_S0[:, h * P:(h + 1) * P], s_S0[:, h * P:(h + 1) * P],
                                            pbank(SU[j], P), ALU.add), dv)
                    last_supd_add[j] = t.count[dv]
                    t.wait(vec, dv, t.count[dv])
                    t.inc(vec.tensor_copy(s_Sbf[:, j * P:(j + 1) * P],
                                          s_S0[:, h * P:(h + 1) * P]), dv)
                    sbf_ev[j] = t.count[dv]
        emit_ffn_conv(2)
        emit_wout_conv(2)
    e_scan_pe = t.count[pe]
    emit_wout_conv(NCH)
    e_woutcv_dv = t.count[dv]
    e_scan_av = t.count[av]
    e_scan_dv = t.count[dv]
    emit_ffn_conv(1000)
    assert ffn_cv["i"] == NE * 3 * NHC, ffn_cv["i"]
    e_ffncv_st = (t.count["dD0"], t.count["dD1"])

    x1_dma = {}
    resid_ev = {}
    ln2T_ev = {}
    ln2_apply = {}
    g_ev = {}
    wostr_used = {}
    tp2_ev = 0
    lgt_ev = 0
    t.wait(sync, pe, qkv_mm_done[(5, 1)])
    t.inc(sync.dma_start(out=s_bout[0:1, :], in_=d_biasb[0:1, 3 * C:4 * C]), dR, 16)
    e_bout_ld = t.count[dR]
    for tpi in range(4):
        ttA, ttB = 2 * tpi, 2 * tpi + 1
        for j, tt in enumerate((ttA, ttB)):
            par = tt % 2
            if tpi == 0:
                t.wait(sync, av, own_apply[NT - 2 + par])
                t.wait(sync, dv, own_stats[NT - 2 + par])
            else:
                t.wait(sync, dv, resid_ev[tt - 2])
                t.wait(sync, av, ln2_apply[tt - 2])
            t.inc(sync.dma_start(out=s_xstr[:, par * C:(par + 1) * C],
                                 in_=d_x[tt * P:(tt + 1) * P, :]), dXp[par], 16)
            if j == 0:
                x_evA = t.count[dXp[par]]
            else:
                x_evB = t.count[dXp[par]]
        for i in range(NCH):
            wpar = i % 2
            if tpi == 0 and i < 2:
                t.wait(sync, dVw[0], wo_st[NCH - 2])
                t.wait(sync, dVw[1], wo_st[NCH - 1])
                t.wait(sync, pe, e_scan_pe)
                t.wait(sync, dv, e_scan_dv)
            if (tpi * NCH + i) >= 2:
                t.wait(sync, pe, wostr_used[tpi * NCH + i - 2])
            t.inc(sync.dma_start(out=s_wostr[:, wpar * C:(wpar + 1) * C],
                                 in_=d_woutb[i * P:(i + 1) * P, :]), dGp[wpar], 16)
            lev = t.count[dGp[wpar]]
            t.wait(ten, dGp[wpar], lev)
            if i == 0:
                t.wait(ten, av, e_scan_av)
                t.wait(ten, dv, e_scan_dv)
                if tpi > 0:
                    t.wait(ten, dv, resid_ev[ttB - 2])
                    t.wait(ten, av, lgt_ev)
            lhsA = s_aT[:, i * KB + ttA * P: i * KB + (ttA + 1) * P]
            lhsB = s_aT[:, i * KB + ttB * P: i * KB + (ttB + 1) * P]
            for b in range(4):
                ten.matmul(pbank(b, 512), lhsA,
                           s_wostr[:, wpar * C + b * 512: wpar * C + (b + 1) * 512],
                           start=(i == 0), stop=False)
            for b in range(4):
                mm = ten.matmul(pbank(4 + b, 512), lhsB,
                                s_wostr[:, wpar * C + b * 512: wpar * C + (b + 1) * 512],
                                start=(i == 0), stop=False)
            wostr_used[tpi * NCH + i] = t.inc(mm, pe)
        t.wait(ten, dR, e_bout_ld)
        for b in range(4):
            ten.matmul(pbank(b, 512), s_onesb[:], s_bout[0:1, b * 512:(b + 1) * 512],
                       start=False, stop=True)
        for b in range(4):
            mm = ten.matmul(pbank(4 + b, 512), s_onesb[:], s_bout[0:1, b * 512:(b + 1) * 512],
                            start=False, stop=True)
        e_pair_mm = t.inc(mm, pe)
        for j, tt in enumerate((ttA, ttB)):
            par = tt % 2
            bset = 4 * j
            t.wait(vec, pe, e_pair_mm)
            t.wait(vec, dXp[par], x_evA if j == 0 else x_evB)
            if tpi > 0:
                t.wait(vec, dOp[par], x1_dma[tt - 2])
                t.wait(vec, av, ln2_apply[tt - 2])
            for b in range(4):
                t.inc(vec.tensor_tensor(s_x1t[:, par * C + b * 512: par * C + (b + 1) * 512],
                                        pbank(bset + b, 512),
                                        s_xstr[:, par * C + b * 512: par * C + (b + 1) * 512],
                                        ALU.add), dv)
            resid_ev[tt] = t.count[dv]
            t.wait(sync, dv, resid_ev[tt])
            t.inc(sync.dma_start(out=d_x1[tt * P:(tt + 1) * P, :],
                                 in_=s_x1t[:, par * C:(par + 1) * C]), dOp[par], 16)
            x1_dma[tt] = t.count[dOp[par]]
            x1sl = s_x1t[:, par * C:(par + 1) * C]
            t.wait(vec, dv, resid_ev[tt])
            _, ln2_apply[tt] = emit_ln_tile(x1sl, par,
                                            ln2T_ev[tt - 2][0] if tt >= 2 else e_scan_pe)
            t.wait(ten, av, ln2_apply[tt])
            if tt == 0:
                t.wait(ten, dv, e_wg)
                t.wait(scal, dv, e_woutcv_dv)
                t.wait(scal, dVw[0], t.count["dVw0"])
                t.wait(scal, dVw[1], t.count["dVw1"])
            t.wait(ten, dv, resid_ev[tt])
            gbk = bset + 2
            for ch in range(NCH):
                bk = bset + ch % 2
                if tp2_ev:
                    t.wait(ten, av, tp2_ev)
                t.inc(ten.transpose(pbankb(bk, P),
                                    s_lntile[:, par * C + ch * P: par * C + (ch + 1) * P],
                                    s_ident[:]), pe)
                t.wait(scal, pe, t.count[pe])
                tp2_ev = t.inc(scal.copy(s_lnT[:, ch * KB + tt * P: ch * KB + (tt + 1) * P],
                                         pbankb(bk, P)), av)
                t.wait(ten, av, tp2_ev)
                if ch == 0 and lgt_ev:
                    t.wait(ten, av, lgt_ev)
                gmm = ten.matmul(pbank(gbk, NE), s_lnT[:, ch * KB + tt * P: ch * KB + (tt + 1) * P],
                                 s_wg[:, ch * NE:(ch + 1) * NE], start=(ch == 0), stop=(ch == NCH - 1))
            gpe = t.inc(gmm, pe)
            ln2T_ev[tt] = (t.count[pe], t.count[av])
            t.wait(scal, pe, gpe)
            if tt > 0:
                t.wait(scal, dv, g_ev[tt - 1])
            lgt_ev = t.inc(scal.copy(s_lgt[:], pbank(gbk, NE)), av)
            t.wait(vec, av, lgt_ev)
            t.inc(vec.max(s_max8[:], s_lgt[:]), dv)
            t.wait(scal, dv, t.count[dv])
            t.inc(scal.activation(s_dx1[:], s_max8[:, 0:1], AF.Exp, bias=s_max8[:, 1:2], scale=-1.0), av)
            t.wait(vec, av, t.count[av])
            t.inc(vec.tensor_scalar(s_w1[:], s_dx1[:], 1.0, None, ALU.add), dv)
            t.wait(vec, dv, t.count[dv])
            t.inc(vec.reciprocal(s_w1[:], s_w1[:]), dv)
            t.wait(vec, dv, t.count[dv])
            t.inc(vec.tensor_mul(s_w2[:], s_dx1[:], s_w1[:]), dv)
            t.wait(vec, dv, t.count[dv])
            t.inc(vec.tensor_scalar(s_g1[:], s_lgt[:], s_max8[:, 0:1], s_w1[:], ALU.is_equal, ALU.mult), dv)
            t.wait(vec, dv, t.count[dv])
            t.inc(vec.tensor_scalar(s_g2[:], s_lgt[:], s_max8[:, 1:2], s_w2[:], ALU.is_equal, ALU.mult), dv)
            t.wait(vec, dv, t.count[dv])
            t.inc(vec.tensor_add(s_g[:, tt * NE:(tt + 1) * NE], s_g1[:], s_g2[:]), dv)
            g_ev[tt] = t.count[dv]
    e_p7_pe = t.count[pe]
    e_p7_av = t.count[av]
    e_p7_dv = t.count[dv]

    seed_ev = {}
    for tt in range(NT):
        if tt == 0:
            t.wait(sync, pe, e_p7_pe)
            t.wait(sync, av, e_p7_av)
            t.wait(sync, dv, e_p7_dv)
            t.wait(sync, dOp[0], x1_dma[NT - 2])
            t.wait(sync, dOp[1], x1_dma[NT - 1])
            t.wait(sync, dDp[0], e_ffncv_st[0])
            t.wait(sync, dDp[1], e_ffncv_st[1])
        t.inc(sync.dma_start(out=s_out[:, tt * C:(tt + 1) * C], in_=d_x1[tt * P:(tt + 1) * P, :]), dS, 16)
        seed_ev[tt] = t.count[dS]

    wi_dma = {}
    f1_grp = {}
    silu_ev = {}
    inter_ev = {}
    wo_dma = {}
    y_evac = {}
    f2_mm = {}

    def emit_wi_load(e, hc):
        par = hc % 2
        idx = e * NHC + hc
        if idx >= 2:
            pidx = idx - 2
            t.wait(scal, pe, f1_grp[(pidx // NHC, pidx % NHC)])
        else:
            t.wait(scal, dv, e_p7_dv)
            t.wait(scal, av, e_p7_av)
            t.wait(scal, dDp[0], e_ffncv_st[0])
            t.wait(scal, dDp[1], e_ffncv_st[1])
        r0 = e * 3 * M + hc * P
        t.inc(scal.dma_start(out=s_wi[:, par * C:(par + 1) * C], in_=d_wall[r0:r0 + P, :]), dFp[par], 16)
        r1 = e * 3 * M + M + hc * P
        t.inc(scal.dma_start(out=s_wi[:, (2 + par) * C:(3 + par) * C], in_=d_wall[r1:r1 + P, :]), dFp[par], 16)
        wi_dma[(e, hc)] = (par, t.count[dFp[par]])

    def emit_wo_load(e, cs):
        key = e * 2 + cs
        par = key % 2
        if key >= 2:
            t.wait(sync, pe, f2_mm[(e - 1, NT - 1, cs)])
        else:
            t.wait(sync, pe, e_p7_pe)
            t.wait(sync, dDp[0], e_ffncv_st[0])
            t.wait(sync, dDp[1], e_ffncv_st[1])
        r0 = e * 3 * M + 2 * M
        src = d_wall[r0:r0 + M, cs * KB:(cs + 1) * KB].rearrange("(h p) n -> p h n", p=P)
        dst = s_wo[:, par * NHC * KB:(par + 1) * NHC * KB].rearrange("p (h n) -> p h n", n=KB)
        t.inc(sync.dma_start(out=dst, in_=src), dGp[par], 16)
        wo_dma[(e, cs)] = (par, t.count[dGp[par]])

    emit_wi_load(0, 0)
    emit_wi_load(0, 1)
    emit_wo_load(0, 0)
    emit_wo_load(0, 1)

    for e in range(NE):
        for hc in range(NHC):
            par = hc % 2
            bset = 4 * par
            t.wait(ten, dFp[wi_dma[(e, hc)][0]], wi_dma[(e, hc)][1])
            if hc >= 2:
                t.wait(ten, av, silu_ev[(e, hc - 2)])
                t.wait(ten, dv, inter_ev[(e, hc - 2)])
            elif e >= 1:
                t.wait(ten, av, silu_ev[(e - 1, NHC - 2 + hc)])
                t.wait(ten, dv, inter_ev[(e - 1, NHC - 2 + hc)])
                t.wait(ten, dv, y_evac[(e - 1, NT - 1, 1)])
            else:
                t.wait(ten, av, e_p7_av)
                t.wait(ten, dv, e_p7_dv)
            wi0 = s_wi[:, par * C:(par + 1) * C]
            wi1 = s_wi[:, (2 + par) * C:(3 + par) * C]
            for ch in range(NCH):
                st_ = (ch == 0); sp = (ch == NCH - 1)
                lhs0 = wi0[:, ch * P:(ch + 1) * P]
                lhs1 = wi1[:, ch * P:(ch + 1) * P]
                rhs0 = s_lnT[:, ch * KB: ch * KB + 512]
                rhs1 = s_lnT[:, ch * KB + 512: (ch + 1) * KB]
                ten.matmul(pbank(bset + 0, 512), lhs0, rhs0, start=st_, stop=sp)
                ten.matmul(pbank(bset + 1, 512), lhs0, rhs1, start=st_, stop=sp)
                ten.matmul(pbank(bset + 2, 512), lhs1, rhs0, start=st_, stop=sp)
                mm = ten.matmul(pbank(bset + 3, 512), lhs1, rhs1, start=st_, stop=sp)
            f1_grp[(e, hc)] = t.inc(mm, pe)
            nxt_idx = e * NHC + hc + 2
            if nxt_idx < NE * NHC:
                emit_wi_load(nxt_idx // NHC, nxt_idx % NHC)
            t.wait(scal, pe, f1_grp[(e, hc)])
            if hc >= 2:
                t.wait(scal, dv, inter_ev[(e, hc - 2)])
            elif e >= 1:
                t.wait(scal, dv, inter_ev[(e - 1, NHC - 2 + hc)])
            t.inc(scal.activation(s_silu[:, par * KB: par * KB + 512], pbank(bset + 0, 512), AF.Silu), av)
            t.inc(scal.activation(s_silu[:, par * KB + 512:(par + 1) * KB], pbank(bset + 1, 512), AF.Silu), av)
            silu_ev[(e, hc)] = t.count[av]
            t.wait(vec, av, silu_ev[(e, hc)])
            if e >= 1 and hc == 0:
                t.wait(vec, pe, f2_mm[(e - 1, NT - 1, 1)])
            ib = hc * KB
            t.inc(vec.tensor_tensor(s_inter[:, ib:ib + 512],
                                    s_silu[:, par * KB: par * KB + 512], pbank(bset + 2, 512), ALU.mult), dv)
            t.inc(vec.tensor_tensor(s_inter[:, ib + 512:ib + KB],
                                    s_silu[:, par * KB + 512:(par + 1) * KB], pbank(bset + 3, 512),
                                    ALU.mult), dv)
            inter_ev[(e, hc)] = t.count[dv]
        ybanks = (2, 3)
        for cs in range(2):
            wpar = wo_dma[(e, cs)][0]
            wosl = s_wo[:, wpar * NHC * KB:(wpar + 1) * NHC * KB]
            t.wait(ten, dGp[wpar], wo_dma[(e, cs)][1])
            for tt in range(NT):
                if tt == 0 and cs == 0:
                    t.wait(ten, av, silu_ev[(e, NHC - 1)])
                    t.wait(ten, dv, inter_ev[(e, NHC - 1)])
                    t.wait(ten, dv, inter_ev[(e, NHC - 2)])
                if (e, tt - 1, cs) in y_evac:
                    t.wait(ten, dv, y_evac[(e, tt - 1, cs)])
                elif tt == 0 and cs == 1:
                    t.wait(ten, dv, y_evac[(e, NT - 1, 0)])
                for hc in range(NHC):
                    st_ = (hc == 0); sp = (hc == NHC - 1)
                    ib = hc * KB
                    lhs = s_inter[:, ib + tt * P: ib + (tt + 1) * P]
                    ten.matmul(pbank(ybanks[0], 512), lhs, wosl[:, hc * KB: hc * KB + 512],
                               start=st_, stop=sp)
                    mm = ten.matmul(pbank(ybanks[1], 512), lhs, wosl[:, hc * KB + 512:(hc + 1) * KB],
                                    start=st_, stop=sp)
                f2_mm[(e, tt, cs)] = t.inc(mm, pe)
                t.wait(vec, pe, f2_mm[(e, tt, cs)])
                if e == 0 and cs == 0:
                    t.wait(vec, dS, seed_ev[NT - 1])
                    t.wait(vec, dv, g_ev[tt])
                if e > 0:
                    t.wait(vec, dv, y_evac[(e - 1, tt, cs)])
                ob = tt * C + cs * KB
                gsc = s_g[:, tt * NE + e: tt * NE + e + 1]
                t.inc(vec.scalar_tensor_tensor(s_out[:, ob:ob + 512], pbank(ybanks[0], 512), gsc,
                                               s_out[:, ob:ob + 512], ALU.mult, ALU.add), dv)
                t.inc(vec.scalar_tensor_tensor(s_out[:, ob + 512:ob + KB], pbank(ybanks[1], 512), gsc,
                                               s_out[:, ob + 512:ob + KB], ALU.mult, ALU.add), dv)
                y_evac[(e, tt, cs)] = t.count[dv]
            if e + 1 < NE:
                emit_wo_load(e + 1, cs)

    for tt in range(NT):
        t.wait(sync, dv, y_evac[(NE - 1, tt, 1)])
        t.inc(sync.dma_start(out=d_y[tt * P:(tt + 1) * P, :], in_=s_out[:, tt * C:(tt + 1) * C]),
              dOp[tt % 2], 16)

    if DEBUG_TAPS:
        t.wait(sync, av, t.count[av])
        t.wait(sync, dv, t.count[dv])
        t.inc(sync.dma_start(out=taps["tap_g"], in_=s_g[:]), dOp[0], 16)
        t.inc(sync.dma_start(out=taps["tap_x1"], in_=d_x1), dOp[0], 16)
        t.inc(sync.dma_start(out=taps["tap_S0"], in_=s_S0[:]), dOp[0], 16)

    t.wait(sync, dOp[0], t.count["dO0"])
    t.wait(sync, dOp[1], t.count["dO1"])

    stack.close()
    return nc


_PROG_CACHE = {}


def _get_program():
    key = ("prog", DEBUG_TAPS)
    if key not in _PROG_CACHE:
        _PROG_CACHE[key] = build_program()
    return _PROG_CACHE[key]


def make_in_maps(L0_attn_x, L0_attn_Wqkv, L0_attn_bqkv, L0_attn_Wout, L0_attn_bout,
                 L0_attn_slopes, L0_ffn_w_gate, L0_ffn_wi_0, L0_ffn_wi_1, L0_ffn_wo):
    f32 = np.float32
    x = np.ascontiguousarray(np.asarray(L0_attn_x, f32)).reshape(2, 4, KB, C)
    wqkv = np.ascontiguousarray(np.asarray(L0_attn_Wqkv, f32))
    bqkv = np.ascontiguousarray(np.asarray(L0_attn_bqkv, f32))
    wout = np.ascontiguousarray(np.asarray(L0_attn_Wout, f32))
    bout = np.ascontiguousarray(np.asarray(L0_attn_bout, f32))
    slopes = np.ascontiguousarray(np.asarray(L0_attn_slopes, f32))
    wg = np.asarray(L0_ffn_w_gate, f32)
    wi0 = np.asarray(L0_ffn_wi_0, f32)
    wi1 = np.asarray(L0_ffn_wi_1, f32)
    wo = np.asarray(L0_ffn_wo, f32)

    wg_im = np.ascontiguousarray(wg.reshape(NCH, P, NE).transpose(1, 0, 2).reshape(P, NCH * NE))
    ident = np.eye(P, dtype=np.float32).astype(ml_dtypes.bfloat16)
    jj = np.arange(P, dtype=f32)
    tri = np.ascontiguousarray((jj[None, :] >= jj[:, None]).astype(f32))
    cols = np.ascontiguousarray(
        np.stack([jj, P - jj, np.full(P, float(P), f32)], axis=1).astype(f32))
    tsc = (P - jj)[:, None] + P * (NT - 1 - np.arange(NT, dtype=f32))[None, :]
    tsc = np.ascontiguousarray(tsc.astype(f32))
    iota_row = jj[None, :]
    rows = np.concatenate([iota_row, np.zeros((1, P), f32)], axis=1).astype(f32)

    def wi_im(w):
        return np.ascontiguousarray(
            w.reshape(NCH, P, NHC, P).transpose(2, 1, 0, 3).reshape(M, C))

    blocks = []
    for e in range(NE):
        blocks.append(wi_im(wi0[e]))
        blocks.append(wi_im(wi1[e]))
        blocks.append(np.ascontiguousarray(wo[e]))
    wffn_im = np.ascontiguousarray(np.concatenate(blocks, axis=0))

    in_maps = []
    for r in range(8):
        b, s = r // 4, r % 4
        xp = np.zeros((NPT * P, C), f32)
        if s > 0:
            pref = x[b, 0:s].reshape(s * KB, C)
            xp[NPT * P - s * KB:] = pref
        nvalid = s * 8
        ptsc = np.full((P, NPT), 1e9, f32)
        for pt in range(NPT - nvalid, NPT):
            ptsc[:, pt] = (P - jj) + P * (NPT - 1 - pt)
        in_maps.append({
            "xs": np.ascontiguousarray(x[b, s]),
            "xp": xp,
            "wqkv": wqkv, "bqkv": bqkv, "wout": wout, "bout": bout,
            "slopes": slopes, "wg_im": wg_im, "wffn_im": wffn_im,
            "c_ident": ident, "c_tri": tri, "c_cols": cols, "c_tsc": tsc,
            "c_ptsc": np.ascontiguousarray(ptsc),
            "c_rows": rows,
        })
    return in_maps


def kernel(L0_attn_x, L0_attn_Wqkv, L0_attn_bqkv, L0_attn_Wout, L0_attn_bout,
           L0_attn_slopes, L0_ffn_w_gate, L0_ffn_wi_0, L0_ffn_wi_1, L0_ffn_wo):
    nc = _get_program()
    in_maps = make_in_maps(L0_attn_x, L0_attn_Wqkv, L0_attn_bqkv, L0_attn_Wout, L0_attn_bout,
                           L0_attn_slopes, L0_ffn_w_gate, L0_ffn_wi_0, L0_ffn_wi_1, L0_ffn_wo)
    res = run_bass_kernel_spmd(nc, in_maps, core_ids=list(range(8)))
    shards = [res.results[r]["y"] for r in range(8)]
    out = np.stack(shards, axis=0).reshape(2, 4 * KB, C).astype(np.float32)
    return out


# version 09
# speedup vs baseline: 1.1016x; 1.0737x over previous
import contextlib
import numpy as np
import ml_dtypes

import concourse.bass as bass
import concourse.mybir as mybir
from concourse.bass_utils import run_bass_kernel_spmd

F32 = mybir.dt.float32
BF16 = mybir.dt.bfloat16
AF = mybir.ActivationFunctionType
ALU = mybir.AluOpType

P = 128
NT = 8
NPT = 24
H = 16
C = 2048
NCH = 16
NE = 8
M = 1024
NHC = 8
KB = 1024
KiB = 1024

DEBUG_TAPS = False


class Tracker:
    def __init__(self, nc, stack):
        self.nc = nc
        self.count = {}
        self.handle = {}
        self.stack = stack

    def sem(self, name):
        self.handle[name] = self.stack.enter_context(self.nc.semaphore(name))
        self.count[name] = 0
        return name

    def inc(self, inst, s, n=1):
        self.count[s] += n
        inst.then_inc(self.handle[s], n)
        return self.count[s]

    def wait(self, eng, s, val=None):
        v = self.count[s] if val is None else val
        assert v <= self.count[s], f"wait on {s} for future value {v} > {self.count[s]}"
        if v > 0:
            eng.wait_ge(self.handle[s], v)


def build_program():
    nc = bass.Bass("TRN2", target_bir_lowering=False)
    stack = contextlib.ExitStack()
    t = Tracker(nc, stack)

    d_x = nc.dram_tensor("xs", [KB, C], F32, kind="ExternalInput").ap()
    d_xp = nc.dram_tensor("xp", [NPT * P, C], F32, kind="ExternalInput").ap()
    d_wqkv = nc.dram_tensor("wqkv", [C, 3 * C], F32, kind="ExternalInput").ap()
    d_bqkv = nc.dram_tensor("bqkv", [3 * C], F32, kind="ExternalInput").ap()
    d_wout = nc.dram_tensor("wout", [C, C], F32, kind="ExternalInput").ap()
    d_bout = nc.dram_tensor("bout", [C], F32, kind="ExternalInput").ap()
    d_slopes = nc.dram_tensor("slopes", [H], F32, kind="ExternalInput").ap()
    d_wg = nc.dram_tensor("wg_im", [P, NCH * NE], F32, kind="ExternalInput").ap()
    d_wffn = nc.dram_tensor("wffn_im", [NE * 3 * M, C], F32, kind="ExternalInput").ap()
    d_ident = nc.dram_tensor("c_ident", [P, P], BF16, kind="ExternalInput").ap()
    d_tri = nc.dram_tensor("c_tri", [P, P], F32, kind="ExternalInput").ap()
    d_cols = nc.dram_tensor("c_cols", [P, 3], F32, kind="ExternalInput").ap()
    d_tsc = nc.dram_tensor("c_tsc", [P, NT], F32, kind="ExternalInput").ap()
    d_ptsc = nc.dram_tensor("c_ptsc", [P, NPT], F32, kind="ExternalInput").ap()
    d_rows = nc.dram_tensor("c_rows", [1, 2 * P], F32, kind="ExternalInput").ap()
    d_y = nc.dram_tensor("y", [KB, C], F32, kind="ExternalOutput").ap()

    d_wqkvb = nc.dram_tensor("wqkvb", [C, 3 * C], BF16, kind="Internal").ap()
    d_wall = nc.dram_tensor("wall", [NE * 3 * M, C], BF16, kind="Internal").ap()
    d_woutb = nc.dram_tensor("woutb", [C, C], BF16, kind="Internal").ap()
    d_x1 = nc.dram_tensor("x1d", [KB, C], F32, kind="Internal").ap()
    d_biasb = nc.dram_tensor("biasb", [1, 4 * C], BF16, kind="Internal").ap()

    taps = {}
    if DEBUG_TAPS:
        taps["tap_g"] = nc.dram_tensor("tap_g", [P, NT * NE], F32, kind="ExternalOutput").ap()
        taps["tap_x1"] = nc.dram_tensor("tap_x1", [KB, C], F32, kind="ExternalOutput").ap()
        taps["tap_S0"] = nc.dram_tensor("tap_S0", [P, H * P], F32, kind="ExternalOutput").ap()

    B = 18 * KiB

    def at(name, shape, dtype, off):
        assert off % 32 == 0
        return nc.alloc_sbuf_tensor_at(name, shape, dtype, offset=B + off)

    o = 0

    def nxt(name, shape, dtype):
        nonlocal o
        h = at(name, shape, dtype, o)
        sz = int(np.prod(shape[1:])) * (2 if dtype == BF16 else 4)
        o += (sz + 31) // 32 * 32
        return h

    s_decay = nxt("s_decay", [P, H * P], F32)
    s_bout = nxt("s_bout", [1, 2 * KB], BF16)
    s_qdec = nxt("s_qdec", [P, H], F32)
    s_kdec = nxt("s_kdec", [P, H], F32)
    s_dcoef = nxt("s_dcoef", [P, H], F32)
    s_negs = nxt("s_negs", [P, H], F32)
    s_tri = nxt("s_tri", [P, P], F32)
    s_cols = nxt("s_cols", [P, 4], F32)
    s_tsc = nxt("s_tsc", [P, NT], F32)
    s_ptsc = nxt("s_ptsc", [P, NPT], F32)
    s_rs = nxt("s_rs", [P, NT], F32)
    s_ident = nxt("s_ident", [P, P], BF16)
    s_rows32 = nxt("s_rows32", [1, 2 * P], F32)
    s_srow = nxt("s_srow", [1, P], F32)
    s_nsrow = nxt("s_nsrow", [1, P], F32)
    s_wg = nxt("s_wg", [P, NCH * NE], BF16)
    s_Sbf = nxt("s_Sbf", [P, 2 * P], BF16)
    s_qT = nxt("s_qT", [P, 2 * P], BF16)
    s_qdT = nxt("s_qdT", [P, 2 * P], BF16)
    s_kT = nxt("s_kT", [P, 2 * P], BF16)
    s_scm = nxt("s_scm", [P, 2 * P], BF16)
    s_kbd = nxt("s_kbd", [P, 2 * P], BF16)
    s_qbd = nxt("s_qbd", [P, 2 * P], BF16)
    s_scr32 = at("s_scr32", [P, KB], F32, 85 * KiB)
    s_silu = at("s_silu", [P, 2 * KB], BF16, 173 * KiB)
    s_dmm = s_scr32.ap()[:, 0:P]
    s_wgf = s_scr32.ap()[:, P:2 * P]
    s_stat = nxt("s_stat", [P, 48], F32)
    s_aggr = nxt("s_aggr", [P, 4], F32)
    s_rstd = nxt("s_rstd", [P, 2], F32)
    s_nmr = nxt("s_nmr", [P, 2], F32)
    s_eps = nxt("s_eps", [P, 1], F32)
    s_lgt = nxt("s_lgt", [P, NE], F32)
    s_max8 = nxt("s_max8", [P, NE], F32)
    s_dx1 = nxt("s_dx1", [P, 1], F32)
    s_w1 = nxt("s_w1", [P, 1], F32)
    s_w2 = nxt("s_w2", [P, 1], F32)
    s_g1 = nxt("s_g1", [P, NE], F32)
    s_g2 = nxt("s_g2", [P, NE], F32)
    s_g = nxt("s_g", [P, NT * NE], F32)
    s_onesb = nxt("s_onesb", [1, P], BF16)
    s_onesf = nxt("s_onesf", [1, P], F32)
    s_S0 = nxt("s_S0", [P, H * P], F32)
    assert o <= 29 * KiB, o

    s_lnT = at("s_lnT", [P, NCH * KB], BF16, 29 * KiB)
    s_q = at("s_q", [P, NT * C], BF16, 61 * KiB)
    s_k = at("s_k", [P, NT * C], BF16, 93 * KiB)
    s_v = at("s_v", [P, NT * C], BF16, 125 * KiB)
    s_cvf = at("s_cvf", [P, 2 * C], F32, 61 * KiB)
    s_cvb = at("s_cvb", [P, 2 * C], BF16, 77 * KiB)
    s_wqcf = at("s_wqcf", [P, 2 * 3 * C], F32, 61 * KiB)
    s_wqcb = at("s_wqcb", [P, 2 * 3 * C], BF16, 125 * KiB)
    s_bst = at("s_bst", [1, C], F32, 109 * KiB)
    s_rowsbf = at("s_rowsbf", [1, C], BF16, 117 * KiB)
    s_kvst = at("s_kvst", [P, 4 * 4096], BF16, 93 * KiB)
    s_lntile = at("s_lntile", [P, 2 * C], BF16, 85 * KiB)
    s_x1t = at("s_x1t", [P, 2 * C], F32, 93 * KiB)
    s_wostr = at("s_wostr", [P, 2 * C], BF16, 109 * KiB)
    s_out = at("s_out", [P, NT * C], F32, 61 * KiB)
    s_wo = at("s_wo", [P, 2 * NHC * KB], BF16, 125 * KiB)
    s_wcg = at("s_wcg", [P, NCH * KB], BF16, 157 * KiB)
    s_aT = at("s_aT", [P, H * KB], BF16, 157 * KiB)
    s_inter = at("s_inter", [P, NHC * KB], BF16, 157 * KiB)
    s_cvf2 = at("s_cvf2", [P, 2 * C], F32, 29 * KiB)
    s_cvb2 = at("s_cvb2", [P, 2 * C], BF16, 45 * KiB)
    s_xstr = at("s_xstr", [P, 2 * C], F32, 189 * KiB)
    s_wi = at("s_wi", [P, 4 * C], BF16, 189 * KiB)

    pbs = [stack.enter_context(nc.psum_tensor(f"pb{i}", [P, 512], F32)) for i in range(8)]

    def pbank(i, n=512, off=0):
        return pbs[i].ap()[:, off:off + n]

    def pbankb(i, n=1024, off=0):
        return pbs[i].ap().bitcast(BF16)[:, off:off + n]

    pe = t.sem("pe"); av = t.sem("av"); dv = t.sem("dv")
    dC = t.sem("dC")
    dXp = [t.sem("dX0"), t.sem("dX1")]
    dWp = [t.sem("dW0"), t.sem("dW1")]
    dQp = [t.sem("dQ0"), t.sem("dQ1")]
    dQs = [t.sem("dQs0"), t.sem("dQs1")]
    dEp = [t.sem("dE0"), t.sem("dE1")]
    dDp = [t.sem("dD0"), t.sem("dD1")]
    dVw = [t.sem("dVw0"), t.sem("dVw1")]
    dFp = [t.sem("dF0"), t.sem("dF1")]
    dGp = [t.sem("dG0"), t.sem("dG1")]
    dOp = [t.sem("dO0"), t.sem("dO1")]
    dS = t.sem("dS")
    dB = t.sem("dB")
    dR = t.sem("dR")

    sync, scal, vec, ten = nc.sync, nc.scalar, nc.vector, nc.tensor

    t.inc(sync.dma_start(out=s_ident[:], in_=d_ident), dC, 16)
    t.inc(sync.dma_start(out=s_tri[:], in_=d_tri), dC, 16)
    t.inc(sync.dma_start(out=s_cols[:, 0:3], in_=d_cols), dC, 16)
    t.inc(sync.dma_start(out=s_tsc[:], in_=d_tsc), dC, 16)
    t.inc(sync.dma_start(out=s_ptsc[:], in_=d_ptsc), dC, 16)
    t.inc(sync.dma_start(out=s_rows32[:], in_=d_rows), dC, 16)
    t.inc(sync.dma_start(out=s_wgf, in_=d_wg), dC, 16)
    for r in range(8):
        t.inc(sync.dma_start(out=s_srow[0:1, r * H:(r + 1) * H], in_=d_slopes[None, :]), dC, 16)
    e_dc = t.count[dC]
    bias_cvt = 0
    bias_stv = 0
    for bc in range(4):
        if bc > 0:
            t.wait(sync, av, bias_cvt)
        if bc < 3:
            t.inc(sync.dma_start(out=s_bst[0:1, :], in_=d_bqkv[None, bc * C:(bc + 1) * C]), dB, 16)
        else:
            t.inc(sync.dma_start(out=s_bst[0:1, :], in_=d_bout[None, :]), dB, 16)
        t.wait(scal, dB, t.count[dB])
        t.inc(scal.copy(s_rowsbf[0:1, :], s_bst[0:1, :]), av)
        bias_cvt = t.count[av]
        t.wait(sync, av, bias_cvt)
        t.inc(sync.dma_start(out=d_biasb[0:1, bc * C:(bc + 1) * C], in_=s_rowsbf[0:1, :]), dB, 16)
        bias_stv = t.count[dB]
    e_bias_bf = bias_cvt
    e_bias_st = bias_stv
    t.wait(vec, dC, e_dc)
    t.inc(vec.tensor_copy(s_wg[:], s_wgf), dv)
    e_wg = t.count[dv]
    t.inc(vec.memset(s_eps[:], 1e-5), dv)
    t.inc(vec.memset(s_onesb[:], 1.0), dv)
    t.inc(vec.memset(s_onesf[:], 1.0), dv)
    t.inc(vec.tensor_scalar(s_nsrow[0:1, 0:H], s_srow[0:1, 0:H], -1.0, None, ALU.mult), dv)
    e_setup_dv = t.count[dv]

    t.wait(ten, dv, e_setup_dv)
    ten.matmul(pbank(0, H), s_onesf[:], s_nsrow[0:1, 0:H], start=True, stop=True)
    e_bc = t.inc(ten.matmul(pbank(2, P), s_onesf[:], s_rows32[0:1, 0:P], start=True, stop=True), pe)
    t.wait(vec, pe, e_bc)
    t.inc(vec.tensor_copy(s_negs[:], pbank(0, H)), dv)
    t.inc(vec.tensor_scalar(s_dmm, pbank(2, P), s_cols[:, 0:1], None, ALU.subtract), dv)
    t.wait(vec, dv, t.count[dv])
    t.inc(vec.tensor_mul(s_dmm, s_dmm, s_tri[:]), dv)
    e_dmm = t.count[dv]

    t.wait(scal, dv, e_dmm)
    for h in range(H):
        t.inc(scal.activation(s_decay[:, h * P:(h + 1) * P], s_dmm, AF.Exp,
                              scale=s_negs[:, h:h + 1]), av)
        t.inc(scal.activation(s_qdec[:, h:h + 1], s_cols[:, 0:1], AF.Exp,
                              scale=s_negs[:, h:h + 1]), av)
        t.inc(scal.activation(s_kdec[:, h:h + 1], s_cols[:, 1:2], AF.Exp,
                              scale=s_negs[:, h:h + 1]), av)
        t.inc(scal.activation(s_dcoef[:, h:h + 1], s_cols[:, 2:3], AF.Exp,
                              scale=s_negs[:, h:h + 1]), av)
    e_decay_exp = t.count[av]
    t.wait(vec, av, e_decay_exp)
    for h in range(H):
        t.inc(vec.tensor_mul(s_decay[:, h * P:(h + 1) * P],
                             s_decay[:, h * P:(h + 1) * P], s_tri[:]), dv)
    e_decay = t.count[dv]

    wq_cvt = {}
    wq_st = {}
    for i in range(NCH):
        par = i % 2
        if i >= 2:
            t.wait(sync, dv, wq_cvt[i - 2])
        t.inc(sync.dma_start(out=s_wqcf[:, par * 3 * C:(par + 1) * 3 * C],
                             in_=d_wqkv[i * P:(i + 1) * P, :]), dQp[par], 16)
        t.wait(vec, dQp[par], t.count[dQp[par]])
        if i >= 2:
            t.wait(vec, dQs[par], wq_st[i - 2])
        t.inc(vec.tensor_copy(s_wqcb[:, par * 3 * C:(par + 1) * 3 * C],
                              s_wqcf[:, par * 3 * C:(par + 1) * 3 * C]), dv)
        wq_cvt[i] = t.count[dv]
        t.wait(sync, dv, wq_cvt[i])
        t.inc(sync.dma_start(out=d_wqkvb[i * P:(i + 1) * P, :],
                             in_=s_wqcb[:, par * 3 * C:(par + 1) * 3 * C]), dQs[par], 16)
        wq_st[i] = t.count[dQs[par]]
    e_wq_pe_gate = t.count[dv]

    ffn_cv = {"i": 0, "rd": {}, "st": {}}

    def emit_ffn_conv(n):
        for _ in range(n):
            i = ffn_cv["i"]
            if i >= NE * 3 * NHC:
                return
            ffn_cv["i"] = i + 1
            par = i % 2
            if i >= 2:
                t.wait(scal, dv, ffn_cv["rd"][i - 2])
            elif i == 0:
                t.wait(scal, dv, e_wq_pe_gate)
            t.inc(scal.dma_start(out=s_cvf[:, par * C:(par + 1) * C],
                                 in_=d_wffn[i * P:(i + 1) * P, :]), dEp[par], 16)
            t.wait(vec, dEp[par], t.count[dEp[par]])
            if i >= 2:
                t.wait(vec, dDp[par], ffn_cv["st"][i - 2])
            t.inc(vec.tensor_copy(s_cvb[:, par * C:(par + 1) * C],
                                  s_cvf[:, par * C:(par + 1) * C]), dv)
            ffn_cv["rd"][i] = t.count[dv]
            t.wait(scal, dv, ffn_cv["rd"][i])
            t.inc(scal.dma_start(out=d_wall[i * P:(i + 1) * P, :],
                                 in_=s_cvb[:, par * C:(par + 1) * C]), dDp[par], 16)
            ffn_cv["st"][i] = t.count[dDp[par]]

    emit_ffn_conv(24)

    ln_state = {"stats_dv": {}, "apply": {}, "tp": 0}

    def emit_ln_tile(xsl, par, prev_pe_gate):
        stt = s_stat[:, par * 24:par * 24 + 24]
        agg = s_aggr[:, par * 2:par * 2 + 2]
        rst = s_rstd[:, par:par + 1]
        nmr = s_nmr[:, par:par + 1]
        for g in range(4):
            t.inc(vec.bn_stats(stt[:, 6 * g:6 * g + 6], xsl[:, 512 * g:512 * (g + 1)]), dv)
        t.wait(vec, dv, t.count[dv])
        t.inc(vec.bn_aggr(agg, stt), dv)
        e_ag = t.count[dv]
        t.wait(scal, dv, e_ag)
        t.inc(scal.activation(rst, agg[:, 1:2], AF.Sqrt, bias=s_eps[:]), av)
        t.wait(vec, av, t.count[av])
        t.inc(vec.reciprocal(rst, rst), dv)
        t.wait(vec, dv, t.count[dv])
        t.inc(vec.tensor_scalar(nmr, agg[:, 0:1], rst, -1.0, ALU.mult, ALU.mult), dv)
        e_nmr = t.count[dv]
        t.wait(scal, dv, e_nmr)
        if prev_pe_gate:
            t.wait(scal, pe, prev_pe_gate)
        t.inc(scal.activation(s_lntile[:, par * C:(par + 1) * C], xsl, AF.Identity,
                              bias=nmr, scale=rst), av)
        return t.count[dv], t.count[av]

    def emit_transposes(par, slot, banks=(0, 1)):
        t.wait(ten, av, t.count[av])
        for ch in range(NCH):
            bk = banks[ch % 2]
            if ln_state["tp"]:
                t.wait(ten, av, ln_state["tp"])
            t.inc(ten.transpose(pbankb(bk, P), s_lntile[:, par * C + ch * P: par * C + (ch + 1) * P],
                                s_ident[:]), pe)
            t.wait(scal, pe, t.count[pe])
            ln_state["tp"] = t.inc(scal.copy(s_lnT[:, ch * KB + slot * P: ch * KB + (slot + 1) * P],
                                             pbankb(bk, P)), av)
        return t.count[pe], t.count[av]

    t.wait(vec, dv, e_decay)
    t.inc(vec.memset(s_S0[:], 0.0), dv)
    e_s0_ms = t.count[dv]
    wcg_user = {}
    wcg_ld = {}
    ub_dv = {}
    e_prev_u_dv = 0
    brow_ld = 0
    x_ev = {}
    pfx_lnT_pe = 0
    pfx_u_mm = {}
    e_prev_kv_mm = 0
    e_prefix_pe = 0
    sbf_gate = 0
    for g in range(6):
        for pt4 in range(4):
            pt = g * 4 + pt4
            par = pt % 2
            if pt >= 2:
                t.wait(sync, av, ln_state["apply"].get(pt - 2, 0))
                t.wait(sync, dv, ln_state["stats_dv"].get(pt - 2, 0))
            t.inc(sync.dma_start(out=s_xstr[:, par * C:(par + 1) * C],
                                 in_=d_xp[pt * P:(pt + 1) * P, :]), dXp[par], 16)
            x_ev[pt] = t.count[dXp[par]]
            t.wait(vec, dXp[par], x_ev[pt])
            if pt == 0:
                t.wait(vec, dv, e_s0_ms)
            sd, ap = emit_ln_tile(s_xstr[:, par * C:(par + 1) * C], par,
                                  pfx_lnT_pe if pt >= 2 else 0)
            ln_state["stats_dv"][pt] = sd
            ln_state["apply"][pt] = ap
            pfx_lnT_pe, _ = emit_transposes(par, pt4)
        emit_ffn_conv(28)
        for cg in range(4):
            colbase = C + cg * KB
            t.wait(sync, dB, e_bias_st)
            if brow_ld:
                t.wait(sync, pe, brow_pe_gate)
            t.inc(sync.dma_start(out=s_bout[0:1, 0:KB], in_=d_biasb[0:1, colbase:colbase + KB]), dR, 16)
            brow_ld = t.count[dR]
            for hf in range(2):
                if (g, cg) != (0, 0):
                    t.wait(sync, pe, wcg_user.get(0, 0))
                elif hf == 0:
                    t.wait(sync, dQs[0], wq_st[NCH - 2])
                    t.wait(sync, dQs[1], wq_st[NCH - 1])
                src = d_wqkvb[hf * KB:(hf + 1) * KB, colbase:colbase + KB].rearrange(
                    "(c p) n -> p c n", p=P)
                dst = s_wcg[:, hf * 8 * KB:(hf + 1) * 8 * KB].rearrange("p (c n) -> p c n", n=KB)
                t.inc(sync.dma_start(out=dst, in_=src), dWp[hf], 16)
                wcg_ld[hf] = t.count[dWp[hf]]
            t.wait(ten, av, t.count[av])
            if e_prev_kv_mm:
                t.wait(ten, av, kv_evac_ev)
            if e_prev_u_dv:
                t.wait(ten, dv, e_prev_u_dv)
            t.wait(ten, dWp[0], wcg_ld[0])
            t.wait(ten, dWp[1], wcg_ld[1])
            for i in range(NCH):
                for pt4 in range(4):
                    lhs = s_lnT[:, i * KB + pt4 * P: i * KB + (pt4 + 1) * P]
                    st_ = (i == 0)
                    ten.matmul(pbank(2 * pt4, 512), lhs, s_wcg[:, i * KB:i * KB + 512],
                               start=st_, stop=False)
                    ten.matmul(pbank(2 * pt4 + 1, 512), lhs, s_wcg[:, i * KB + 512:(i + 1) * KB],
                               start=st_, stop=False)
            t.wait(ten, dR, brow_ld)
            for pt4 in range(4):
                ten.matmul(pbank(2 * pt4, 512), s_onesb[:], s_bout[0:1, 0:512],
                           start=False, stop=True)
                mm = ten.matmul(pbank(2 * pt4 + 1, 512), s_onesb[:], s_bout[0:1, 512:KB],
                                start=False, stop=True)
            e_prev_kv_mm = t.inc(mm, pe)
            brow_pe_gate = e_prev_kv_mm
            wcg_user[0] = e_prev_kv_mm
            t.wait(scal, pe, e_prev_kv_mm)
            if g == 0 and cg == 0:
                t.wait(scal, dv, e_wg)
            for pt4 in range(4):
                base = pt4 * 4096 + cg * KB
                t.inc(scal.copy(s_kvst[:, base:base + 512], pbank(2 * pt4, 512)), av)
                t.inc(scal.copy(s_kvst[:, base + 512:base + KB], pbank(2 * pt4 + 1, 512)), av)
            kv_evac_ev = t.count[av]
        t.wait(ten, av, kv_evac_ev)
        for h in range(H):
            par2 = h % 2
            t.wait(scal, av, t.count[av])
            t.inc(scal.activation(s_rs[:, 0:4], s_ptsc[:, g * 4:(g + 1) * 4], AF.Exp,
                                  scale=s_negs[:, h:h + 1]), av)
            t.wait(scal, av, t.count[av])
            if pfx_u_mm.get(par2):
                t.wait(scal, pe, pfx_u_mm[par2])
            for pt4 in range(4):
                dstsl = (s_kbd, s_qbd, s_qT, s_qdT)[pt4][:, par2 * P:(par2 + 1) * P]
                t.inc(scal.activation(dstsl,
                                      s_kvst[:, pt4 * 4096 + h * P: pt4 * 4096 + (h + 1) * P],
                                      AF.Identity, scale=s_rs[:, pt4:pt4 + 1]), av)
            e_ksc = t.count[av]
            t.wait(ten, av, e_ksc)
            if par2 in ub_dv:
                t.wait(ten, dv, ub_dv[par2])
            for pt4 in range(4):
                srcsl = (s_kbd, s_qbd, s_qT, s_qdT)[pt4][:, par2 * P:(par2 + 1) * P]
                mm = ten.matmul(pbank(4 + par2, P), srcsl,
                                s_kvst[:, pt4 * 4096 + 2048 + h * P: pt4 * 4096 + 2048 + (h + 1) * P],
                                start=(pt4 == 0), stop=(pt4 == 3))
            pfx_u_mm[par2] = t.inc(mm, pe)
            t.wait(vec, pe, pfx_u_mm[par2])
            t.wait(vec, dv, t.count[dv])
            t.inc(vec.tensor_tensor(s_S0[:, h * P:(h + 1) * P], s_S0[:, h * P:(h + 1) * P],
                                    pbank(4 + par2, P), ALU.add), dv)
            ub_dv[par2] = t.count[dv]
        e_prev_u_dv = t.count[dv]
        e_prefix_pe = t.count[pe]
        emit_ffn_conv(2)
    e_S0 = t.count[dv]
    e_prefix_av = t.count[av]

    ln_state["tp"] = ln_state["tp"]
    lnT_ev = {}
    own_apply = {}
    for tt in range(NT):
        par = tt % 2
        if tt < 2:
            t.wait(sync, av, ln_state["apply"][NPT - 2 + tt])
            t.wait(sync, dv, ln_state["stats_dv"][NPT - 2 + tt])
        else:
            t.wait(sync, av, own_apply[tt - 2])
            t.wait(sync, dv, own_stats[tt - 2])
        t.inc(sync.dma_start(out=s_xstr[:, par * C:(par + 1) * C],
                             in_=d_x[tt * P:(tt + 1) * P, :]), dXp[par], 16)
        xev = t.count[dXp[par]]
        t.wait(vec, dXp[par], xev)
        sd, ap = emit_ln_tile(s_xstr[:, par * C:(par + 1) * C], par,
                              lnT_ev[tt - 2][0] if tt >= 2 else e_prefix_pe)
        if tt == 0:
            own_stats = {}
        own_stats[tt] = sd
        own_apply[tt] = ap
        lnT_ev[tt] = emit_transposes(par, tt)
    e_ln1_av = t.count[av]
    e_ln1_dv = t.count[dv]

    dest = [(s_q, 0), (s_q, 1), (s_k, 0), (s_k, 1), (s_v, 0), (s_v, 1)]
    qkv_mm_done = {}
    qkv_evac = {}
    for cg in range(6):
        t.wait(sync, dB, e_bias_st)
        t.wait(sync, pe, brow_pe_gate)
        t.inc(sync.dma_start(out=s_bout[0:1, 0:KB], in_=d_biasb[0:1, cg * KB:(cg + 1) * KB]), dR, 16)
        brow_ld = t.count[dR]
        for hf in range(2):
            t.wait(sync, pe, wcg_user[0])
            src = d_wqkvb[hf * KB:(hf + 1) * KB, cg * KB:(cg + 1) * KB].rearrange(
                "(c p) n -> p c n", p=P)
            dst = s_wcg[:, hf * 8 * KB:(hf + 1) * 8 * KB].rearrange("p (c n) -> p c n", n=KB)
            t.inc(sync.dma_start(out=dst, in_=src), dWp[hf], 16)
            wcg_ld[hf] = t.count[dWp[hf]]
        for grp in range(2):
            tts = list(range(grp * 4, grp * 4 + 4))
            t.wait(ten, av, e_ln1_av if cg == 0 else lnT_ev[max(tts)][1])
            if grp == 1:
                t.wait(ten, av, qkv_evac[(cg, 0)])
            elif cg > 0:
                t.wait(ten, av, qkv_evac[(cg - 1, 1)])
            else:
                t.wait(ten, av, e_prefix_av)
                t.wait(ten, dv, e_S0)
            t.wait(ten, dWp[0], wcg_ld[0])
            t.wait(ten, dWp[1], wcg_ld[1])
            for i in range(NCH):
                for j, tt in enumerate(tts):
                    lhs = s_lnT[:, i * KB + tt * P: i * KB + (tt + 1) * P]
                    st_ = (i == 0)
                    ten.matmul(pbank(2 * j, 512), lhs, s_wcg[:, i * KB:i * KB + 512],
                               start=st_, stop=False)
                    ten.matmul(pbank(2 * j + 1, 512), lhs, s_wcg[:, i * KB + 512:(i + 1) * KB],
                               start=st_, stop=False)
            t.wait(ten, dR, brow_ld)
            for j, tt in enumerate(tts):
                ten.matmul(pbank(2 * j, 512), s_onesb[:], s_bout[0:1, 0:512],
                           start=False, stop=True)
                mm = ten.matmul(pbank(2 * j + 1, 512), s_onesb[:], s_bout[0:1, 512:KB],
                                start=False, stop=True)
            qkv_mm_done[(cg, grp)] = t.inc(mm, pe)
            if grp == 1:
                brow_pe_gate = qkv_mm_done[(cg, grp)]
                wcg_user[0] = qkv_mm_done[(cg, grp)]
            t.wait(scal, pe, qkv_mm_done[(cg, grp)])
            if cg == 0 and grp == 0:
                t.wait(scal, dEp[0], t.count["dE0"])
                t.wait(scal, dEp[1], t.count["dE1"])
                t.wait(scal, dDp[0], t.count["dD0"])
                t.wait(scal, dDp[1], t.count["dD1"])
                t.wait(scal, dQs[0], t.count["dQs0"])
                t.wait(scal, dQs[1], t.count["dQs1"])
                t.wait(scal, dVw[0], t.count["dVw0"])
                t.wait(scal, dVw[1], t.count["dVw1"])
            dst, half = dest[cg]
            for j, tt in enumerate(tts):
                base = tt * C + half * KB
                t.inc(scal.copy(dst[:, base:base + 512], pbank(2 * j, 512)), av)
                t.inc(scal.copy(dst[:, base + 512:base + KB], pbank(2 * j + 1, 512)), av)
            qkv_evac[(cg, grp)] = t.count[av]
        emit_ffn_conv(4)
    e_qkv_av = t.count[av]
    e_qkv_pe = t.count[pe]

    wo_st = {}
    wo_cvt = {}
    woc = {"i": 0}

    def emit_wout_conv(n):
        for _ in range(n):
            i = woc["i"]
            if i >= NCH:
                return
            woc["i"] = i + 1
            par = i % 2
            if i >= 2:
                t.wait(sync, dv, wo_cvt[i - 2])
            else:
                t.wait(sync, pe, qkv_mm_done[(5, 1)])
            t.inc(sync.dma_start(out=s_cvf2[:, par * C:(par + 1) * C],
                                 in_=d_wout[i * P:(i + 1) * P, :]), dQp[par], 16)
            t.wait(vec, dQp[par], t.count[dQp[par]])
            if i >= 2:
                t.wait(vec, dVw[par], wo_st[i - 2])
            t.inc(vec.tensor_copy(s_cvb2[:, par * C:(par + 1) * C],
                                  s_cvf2[:, par * C:(par + 1) * C]), dv)
            wo_cvt[i] = t.count[dv]
            t.wait(sync, dv, wo_cvt[i])
            t.inc(sync.dma_start(out=d_woutb[i * P:(i + 1) * P, :],
                                 in_=s_cvb2[:, par * C:(par + 1) * C]), dVw[par], 16)
            wo_st[i] = t.count[dVw[par]]

    last_scm = {0: 0, 1: 0}
    last_oT_evac = {0: 0, 1: 0}
    last_supd_add = {0: 0, 1: 0}
    qbd_tp_mm = {0: 0, 1: 0}
    supd_mm = {0: 0, 1: 0}
    oT_mm_ev = {0: 0, 1: 0}
    sbf_ev = {0: 0, 1: 0}
    last_tp_evac = {4: 0, 5: 0}
    SC = {0: 0, 1: 2}
    OT = {0: 1, 1: 7}
    SU = {0: 3, 1: 6}
    tpbank = [0]
    for hp in range(H // 2):
        for j in (0, 1):
            h = 2 * hp + j
            t.wait(vec, dv, t.count[dv])
            if oT_mm_ev[j]:
                t.wait(vec, pe, oT_mm_ev[j])
            elif hp == 0:
                t.wait(vec, dv, e_S0)
            t.inc(vec.tensor_copy(s_Sbf[:, j * P:(j + 1) * P], s_S0[:, h * P:(h + 1) * P]), dv)
            sbf_ev[j] = t.count[dv]
        for c in range(NT):
            for j in (0, 1):
                h = 2 * hp + j
                qt = s_q[:, c * C + h * P: c * C + (h + 1) * P]
                kt = s_k[:, c * C + h * P: c * C + (h + 1) * P]
                vt = s_v[:, c * C + h * P: c * C + (h + 1) * P]
                if hp == 0 and c == 0 and j == 0:
                    t.wait(scal, av, e_qkv_av)
                    t.wait(scal, pe, e_qkv_pe)
                if qbd_tp_mm[j]:
                    t.wait(scal, pe, qbd_tp_mm[j])
                t.inc(scal.activation(s_qbd[:, j * P:(j + 1) * P], qt, AF.Identity,
                                      scale=s_qdec[:, h:h + 1]), av)
                qbd_ev = t.count[av]
                if supd_mm[j]:
                    t.wait(scal, pe, supd_mm[j])
                t.inc(scal.activation(s_kbd[:, j * P:(j + 1) * P], kt, AF.Identity,
                                      scale=s_kdec[:, h:h + 1]), av)
                kbd_ev = t.count[av]
                if hp == 0 and c == 0 and j == 0:
                    t.wait(ten, av, e_qkv_av)
                    t.wait(ten, dv, e_S0)
                for which, src2, dst_sb, dep_av in (
                        (0, qt, s_qT, 0), (1, None, s_qdT, qbd_ev), (2, kt, s_kT, 0)):
                    bk = 4 + tpbank[0] % 2
                    tpbank[0] += 1
                    if last_tp_evac[bk]:
                        t.wait(ten, av, last_tp_evac[bk])
                    if dep_av:
                        t.wait(ten, av, dep_av)
                        src2 = s_qbd[:, j * P:(j + 1) * P]
                    mmev = t.inc(ten.transpose(pbankb(bk, P), src2, s_ident[:]), pe)
                    if which == 1:
                        qbd_tp_mm[j] = mmev
                    t.wait(scal, pe, mmev)
                    last_tp_evac[bk] = t.inc(scal.copy(dst_sb[:, j * P:(j + 1) * P],
                                                       pbankb(bk, P)), av)
                e_tp_av = t.count[av]
                t.wait(ten, av, e_tp_av)
                if last_scm[j]:
                    t.wait(ten, dv, last_scm[j])
                sc_mm = t.inc(ten.matmul(pbank(SC[j], P), s_kT[:, j * P:(j + 1) * P],
                                         s_qT[:, j * P:(j + 1) * P], start=True, stop=True), pe)
                t.wait(vec, pe, sc_mm)
                t.inc(vec.tensor_tensor(s_scm[:, j * P:(j + 1) * P], pbank(SC[j], P),
                                        s_decay[:, h * P:(h + 1) * P], ALU.mult), dv)
                last_scm[j] = t.count[dv]
                t.wait(ten, dv, last_scm[j])
                if c == 0:
                    t.wait(ten, dv, sbf_ev[j])
                if last_oT_evac[j]:
                    t.wait(ten, av, last_oT_evac[j])
                ten.matmul(pbank(OT[j], P), vt, s_scm[:, j * P:(j + 1) * P], start=True, stop=False)
                oT_mm_ev[j] = t.inc(ten.matmul(pbank(OT[j], P), s_Sbf[:, j * P:(j + 1) * P],
                                               s_qdT[:, j * P:(j + 1) * P],
                                               start=False, stop=True), pe)
                t.wait(scal, pe, oT_mm_ev[j])
                last_oT_evac[j] = t.inc(scal.copy(s_aT[:, h * KB + c * P: h * KB + (c + 1) * P],
                                                  pbank(OT[j], P)), av)
                if c < NT - 1:
                    t.wait(ten, av, kbd_ev)
                    if last_supd_add[j]:
                        t.wait(ten, dv, last_supd_add[j])
                    supd_mm[j] = t.inc(ten.matmul(pbank(SU[j], P), s_kbd[:, j * P:(j + 1) * P], vt,
                                                  start=True, stop=True), pe)
                    t.wait(vec, pe, oT_mm_ev[j])
                    t.wait(vec, dv, t.count[dv])
                    t.inc(vec.tensor_scalar(s_S0[:, h * P:(h + 1) * P], s_S0[:, h * P:(h + 1) * P],
                                            s_dcoef[:, h:h + 1], None, ALU.mult), dv)
                    t.wait(vec, pe, supd_mm[j])
                    t.wait(vec, dv, t.count[dv])
                    t.inc(vec.tensor_tensor(s_S0[:, h * P:(h + 1) * P], s_S0[:, h * P:(h + 1) * P],
                                            pbank(SU[j], P), ALU.add), dv)
                    last_supd_add[j] = t.count[dv]
                    t.wait(vec, dv, t.count[dv])
                    t.inc(vec.tensor_copy(s_Sbf[:, j * P:(j + 1) * P],
                                          s_S0[:, h * P:(h + 1) * P]), dv)
                    sbf_ev[j] = t.count[dv]
        emit_ffn_conv(2)
        emit_wout_conv(2)
    e_scan_pe = t.count[pe]
    emit_wout_conv(NCH)
    e_woutcv_dv = t.count[dv]
    e_scan_av = t.count[av]
    e_scan_dv = t.count[dv]
    emit_ffn_conv(1000)
    assert ffn_cv["i"] == NE * 3 * NHC, ffn_cv["i"]
    e_ffncv_st = (t.count["dD0"], t.count["dD1"])

    x1_dma = {}
    resid_ev = {}
    ln2T_ev = {}
    ln2_apply = {}
    g_ev = {}
    wostr_used = {}
    tp2_ev = 0
    lgt_ev = 0
    t.wait(sync, pe, qkv_mm_done[(5, 1)])
    t.inc(sync.dma_start(out=s_bout[0:1, :], in_=d_biasb[0:1, 3 * C:4 * C]), dR, 16)
    e_bout_ld = t.count[dR]
    for tpi in range(4):
        ttA, ttB = 2 * tpi, 2 * tpi + 1
        for j, tt in enumerate((ttA, ttB)):
            par = tt % 2
            if tpi == 0:
                t.wait(sync, av, own_apply[NT - 2 + par])
                t.wait(sync, dv, own_stats[NT - 2 + par])
            else:
                t.wait(sync, dv, resid_ev[tt - 2])
                t.wait(sync, av, ln2_apply[tt - 2])
            t.inc(sync.dma_start(out=s_xstr[:, par * C:(par + 1) * C],
                                 in_=d_x[tt * P:(tt + 1) * P, :]), dXp[par], 16)
            if j == 0:
                x_evA = t.count[dXp[par]]
            else:
                x_evB = t.count[dXp[par]]
        for i in range(NCH):
            wpar = i % 2
            if tpi == 0 and i < 2:
                t.wait(sync, dVw[0], wo_st[NCH - 2])
                t.wait(sync, dVw[1], wo_st[NCH - 1])
                t.wait(sync, pe, e_scan_pe)
                t.wait(sync, dv, e_scan_dv)
            if (tpi * NCH + i) >= 2:
                t.wait(sync, pe, wostr_used[tpi * NCH + i - 2])
            t.inc(sync.dma_start(out=s_wostr[:, wpar * C:(wpar + 1) * C],
                                 in_=d_woutb[i * P:(i + 1) * P, :]), dGp[wpar], 16)
            lev = t.count[dGp[wpar]]
            t.wait(ten, dGp[wpar], lev)
            if i == 0:
                t.wait(ten, av, e_scan_av)
                t.wait(ten, dv, e_scan_dv)
                if tpi > 0:
                    t.wait(ten, dv, resid_ev[ttB - 2])
                    t.wait(ten, av, lgt_ev)
            lhsA = s_aT[:, i * KB + ttA * P: i * KB + (ttA + 1) * P]
            lhsB = s_aT[:, i * KB + ttB * P: i * KB + (ttB + 1) * P]
            for b in range(4):
                ten.matmul(pbank(b, 512), lhsA,
                           s_wostr[:, wpar * C + b * 512: wpar * C + (b + 1) * 512],
                           start=(i == 0), stop=False)
            for b in range(4):
                mm = ten.matmul(pbank(4 + b, 512), lhsB,
                                s_wostr[:, wpar * C + b * 512: wpar * C + (b + 1) * 512],
                                start=(i == 0), stop=False)
            wostr_used[tpi * NCH + i] = t.inc(mm, pe)
        t.wait(ten, dR, e_bout_ld)
        for b in range(4):
            ten.matmul(pbank(b, 512), s_onesb[:], s_bout[0:1, b * 512:(b + 1) * 512],
                       start=False, stop=True)
        for b in range(4):
            mm = ten.matmul(pbank(4 + b, 512), s_onesb[:], s_bout[0:1, b * 512:(b + 1) * 512],
                            start=False, stop=True)
        e_pair_mm = t.inc(mm, pe)
        for j, tt in enumerate((ttA, ttB)):
            par = tt % 2
            bset = 4 * j
            t.wait(vec, pe, e_pair_mm)
            t.wait(vec, dXp[par], x_evA if j == 0 else x_evB)
            if tpi > 0:
                t.wait(vec, dOp[par], x1_dma[tt - 2])
                t.wait(vec, av, ln2_apply[tt - 2])
            for b in range(4):
                t.inc(vec.tensor_tensor(s_x1t[:, par * C + b * 512: par * C + (b + 1) * 512],
                                        pbank(bset + b, 512),
                                        s_xstr[:, par * C + b * 512: par * C + (b + 1) * 512],
                                        ALU.add), dv)
            resid_ev[tt] = t.count[dv]
            t.wait(sync, dv, resid_ev[tt])
            t.inc(sync.dma_start(out=d_x1[tt * P:(tt + 1) * P, :],
                                 in_=s_x1t[:, par * C:(par + 1) * C]), dOp[par], 16)
            x1_dma[tt] = t.count[dOp[par]]
            x1sl = s_x1t[:, par * C:(par + 1) * C]
            t.wait(vec, dv, resid_ev[tt])
            _, ln2_apply[tt] = emit_ln_tile(x1sl, par,
                                            ln2T_ev[tt - 2][0] if tt >= 2 else e_scan_pe)
            t.wait(ten, av, ln2_apply[tt])
            if tt == 0:
                t.wait(ten, dv, e_wg)
                t.wait(scal, dv, e_woutcv_dv)
                t.wait(scal, dVw[0], t.count["dVw0"])
                t.wait(scal, dVw[1], t.count["dVw1"])
            t.wait(ten, dv, resid_ev[tt])
            gbk = bset + 2
            for ch in range(NCH):
                bk = bset + ch % 2
                if tp2_ev:
                    t.wait(ten, av, tp2_ev)
                t.inc(ten.transpose(pbankb(bk, P),
                                    s_lntile[:, par * C + ch * P: par * C + (ch + 1) * P],
                                    s_ident[:]), pe)
                t.wait(scal, pe, t.count[pe])
                tp2_ev = t.inc(scal.copy(s_lnT[:, ch * KB + tt * P: ch * KB + (tt + 1) * P],
                                         pbankb(bk, P)), av)
                t.wait(ten, av, tp2_ev)
                if ch == 0 and lgt_ev:
                    t.wait(ten, av, lgt_ev)
                gmm = ten.matmul(pbank(gbk, NE), s_lnT[:, ch * KB + tt * P: ch * KB + (tt + 1) * P],
                                 s_wg[:, ch * NE:(ch + 1) * NE], start=(ch == 0), stop=(ch == NCH - 1))
            gpe = t.inc(gmm, pe)
            ln2T_ev[tt] = (t.count[pe], t.count[av])
            t.wait(scal, pe, gpe)
            if tt > 0:
                t.wait(scal, dv, g_ev[tt - 1])
            lgt_ev = t.inc(scal.copy(s_lgt[:], pbank(gbk, NE)), av)
            t.wait(vec, av, lgt_ev)
            t.inc(vec.max(s_max8[:], s_lgt[:]), dv)
            t.wait(scal, dv, t.count[dv])
            t.inc(scal.activation(s_dx1[:], s_max8[:, 0:1], AF.Exp, bias=s_max8[:, 1:2], scale=-1.0), av)
            t.wait(vec, av, t.count[av])
            t.inc(vec.tensor_scalar(s_w1[:], s_dx1[:], 1.0, None, ALU.add), dv)
            t.wait(vec, dv, t.count[dv])
            t.inc(vec.reciprocal(s_w1[:], s_w1[:]), dv)
            t.wait(vec, dv, t.count[dv])
            t.inc(vec.tensor_mul(s_w2[:], s_dx1[:], s_w1[:]), dv)
            t.wait(vec, dv, t.count[dv])
            t.inc(vec.tensor_scalar(s_g1[:], s_lgt[:], s_max8[:, 0:1], s_w1[:], ALU.is_equal, ALU.mult), dv)
            t.wait(vec, dv, t.count[dv])
            t.inc(vec.tensor_scalar(s_g2[:], s_lgt[:], s_max8[:, 1:2], s_w2[:], ALU.is_equal, ALU.mult), dv)
            t.wait(vec, dv, t.count[dv])
            t.inc(vec.tensor_add(s_g[:, tt * NE:(tt + 1) * NE], s_g1[:], s_g2[:]), dv)
            g_ev[tt] = t.count[dv]
    e_p7_pe = t.count[pe]
    e_p7_av = t.count[av]
    e_p7_dv = t.count[dv]

    seed_ev = {}
    for tt in range(NT):
        if tt == 0:
            t.wait(sync, pe, e_p7_pe)
            t.wait(sync, av, e_p7_av)
            t.wait(sync, dv, e_p7_dv)
            t.wait(sync, dOp[0], x1_dma[NT - 2])
            t.wait(sync, dOp[1], x1_dma[NT - 1])
            t.wait(sync, dDp[0], e_ffncv_st[0])
            t.wait(sync, dDp[1], e_ffncv_st[1])
        t.inc(sync.dma_start(out=s_out[:, tt * C:(tt + 1) * C], in_=d_x1[tt * P:(tt + 1) * P, :]), dS, 16)
        seed_ev[tt] = t.count[dS]

    wi_dma = {}
    f1_grp = {}
    silu_ev = {}
    inter_ev = {}
    wo_dma = {}
    y_evac = {}
    f2_mm = {}

    def emit_wi_load(e, hc):
        par = hc % 2
        idx = e * NHC + hc
        if idx >= 2:
            pidx = idx - 2
            t.wait(scal, pe, f1_grp[(pidx // NHC, pidx % NHC)])
        else:
            t.wait(scal, dv, e_p7_dv)
            t.wait(scal, av, e_p7_av)
            t.wait(scal, dDp[0], e_ffncv_st[0])
            t.wait(scal, dDp[1], e_ffncv_st[1])
        r0 = e * 3 * M + hc * P
        t.inc(scal.dma_start(out=s_wi[:, par * C:(par + 1) * C], in_=d_wall[r0:r0 + P, :]), dFp[par], 16)
        r1 = e * 3 * M + M + hc * P
        t.inc(scal.dma_start(out=s_wi[:, (2 + par) * C:(3 + par) * C], in_=d_wall[r1:r1 + P, :]), dFp[par], 16)
        wi_dma[(e, hc)] = (par, t.count[dFp[par]])

    def emit_wo_load(e, cs):
        key = e * 2 + cs
        par = key % 2
        if key >= 2:
            t.wait(sync, pe, f2_mm[(e - 1, NT - 1, cs)])
        else:
            t.wait(sync, pe, e_p7_pe)
            t.wait(sync, dDp[0], e_ffncv_st[0])
            t.wait(sync, dDp[1], e_ffncv_st[1])
        r0 = e * 3 * M + 2 * M
        src = d_wall[r0:r0 + M, cs * KB:(cs + 1) * KB].rearrange("(h p) n -> p h n", p=P)
        dst = s_wo[:, par * NHC * KB:(par + 1) * NHC * KB].rearrange("p (h n) -> p h n", n=KB)
        t.inc(sync.dma_start(out=dst, in_=src), dGp[par], 16)
        wo_dma[(e, cs)] = (par, t.count[dGp[par]])

    emit_wi_load(0, 0)
    emit_wi_load(0, 1)
    emit_wo_load(0, 0)
    emit_wo_load(0, 1)

    for e in range(NE):
        for hc in range(NHC):
            par = hc % 2
            bset = 4 * par
            t.wait(ten, dFp[wi_dma[(e, hc)][0]], wi_dma[(e, hc)][1])
            if hc >= 2:
                t.wait(ten, av, silu_ev[(e, hc - 2)])
                t.wait(ten, dv, inter_ev[(e, hc - 2)])
            elif e >= 1:
                t.wait(ten, av, silu_ev[(e - 1, NHC - 2 + hc)])
                t.wait(ten, dv, inter_ev[(e - 1, NHC - 2 + hc)])
                t.wait(ten, dv, y_evac[(e - 1, NT - 1, 1)])
            else:
                t.wait(ten, av, e_p7_av)
                t.wait(ten, dv, e_p7_dv)
            wi0 = s_wi[:, par * C:(par + 1) * C]
            wi1 = s_wi[:, (2 + par) * C:(3 + par) * C]
            for ch in range(NCH):
                st_ = (ch == 0); sp = (ch == NCH - 1)
                lhs0 = wi0[:, ch * P:(ch + 1) * P]
                lhs1 = wi1[:, ch * P:(ch + 1) * P]
                rhs0 = s_lnT[:, ch * KB: ch * KB + 512]
                rhs1 = s_lnT[:, ch * KB + 512: (ch + 1) * KB]
                ten.matmul(pbank(bset + 0, 512), lhs0, rhs0, start=st_, stop=sp)
                ten.matmul(pbank(bset + 1, 512), lhs0, rhs1, start=st_, stop=sp)
                ten.matmul(pbank(bset + 2, 512), lhs1, rhs0, start=st_, stop=sp)
                mm = ten.matmul(pbank(bset + 3, 512), lhs1, rhs1, start=st_, stop=sp)
            f1_grp[(e, hc)] = t.inc(mm, pe)
            nxt_idx = e * NHC + hc + 2
            if nxt_idx < NE * NHC:
                emit_wi_load(nxt_idx // NHC, nxt_idx % NHC)
            t.wait(scal, pe, f1_grp[(e, hc)])
            if hc >= 2:
                t.wait(scal, dv, inter_ev[(e, hc - 2)])
            elif e >= 1:
                t.wait(scal, dv, inter_ev[(e - 1, NHC - 2 + hc)])
            t.inc(scal.activation(s_silu[:, par * KB: par * KB + 512], pbank(bset + 0, 512), AF.Silu), av)
            t.inc(scal.activation(s_silu[:, par * KB + 512:(par + 1) * KB], pbank(bset + 1, 512), AF.Silu), av)
            silu_ev[(e, hc)] = t.count[av]
            t.wait(vec, av, silu_ev[(e, hc)])
            if e >= 1 and hc == 0:
                t.wait(vec, pe, f2_mm[(e - 1, NT - 1, 1)])
            ib = hc * KB
            t.inc(vec.tensor_tensor(s_inter[:, ib:ib + 512],
                                    s_silu[:, par * KB: par * KB + 512], pbank(bset + 2, 512), ALU.mult), dv)
            t.inc(vec.tensor_tensor(s_inter[:, ib + 512:ib + KB],
                                    s_silu[:, par * KB + 512:(par + 1) * KB], pbank(bset + 3, 512),
                                    ALU.mult), dv)
            inter_ev[(e, hc)] = t.count[dv]
        for cs in range(2):
            wpar = wo_dma[(e, cs)][0]
            wosl = s_wo[:, wpar * NHC * KB:(wpar + 1) * NHC * KB]
            t.wait(ten, dGp[wpar], wo_dma[(e, cs)][1])
            for tt in range(NT):
                ybanks = (2, 3) if tt % 2 == 0 else (6, 7)
                if tt < 2 and cs == 0:
                    t.wait(ten, av, silu_ev[(e, NHC - 1)])
                    t.wait(ten, dv, inter_ev[(e, NHC - 1)])
                    t.wait(ten, dv, inter_ev[(e, NHC - 2)])
                if (e, tt - 2, cs) in y_evac:
                    t.wait(ten, dv, y_evac[(e, tt - 2, cs)])
                elif tt < 2 and cs == 1:
                    t.wait(ten, dv, y_evac[(e, NT - 2 + tt, 0)])
                for hc in range(NHC):
                    st_ = (hc == 0); sp = (hc == NHC - 1)
                    ib = hc * KB
                    lhs = s_inter[:, ib + tt * P: ib + (tt + 1) * P]
                    ten.matmul(pbank(ybanks[0], 512), lhs, wosl[:, hc * KB: hc * KB + 512],
                               start=st_, stop=sp)
                    mm = ten.matmul(pbank(ybanks[1], 512), lhs, wosl[:, hc * KB + 512:(hc + 1) * KB],
                                    start=st_, stop=sp)
                f2_mm[(e, tt, cs)] = t.inc(mm, pe)
                t.wait(vec, pe, f2_mm[(e, tt, cs)])
                if e == 0 and cs == 0:
                    t.wait(vec, dS, seed_ev[NT - 1])
                    t.wait(vec, dv, g_ev[tt])
                if e > 0:
                    t.wait(vec, dv, y_evac[(e - 1, tt, cs)])
                ob = tt * C + cs * KB
                gsc = s_g[:, tt * NE + e: tt * NE + e + 1]
                t.inc(vec.scalar_tensor_tensor(s_out[:, ob:ob + 512], pbank(ybanks[0], 512), gsc,
                                               s_out[:, ob:ob + 512], ALU.mult, ALU.add), dv)
                t.inc(vec.scalar_tensor_tensor(s_out[:, ob + 512:ob + KB], pbank(ybanks[1], 512), gsc,
                                               s_out[:, ob + 512:ob + KB], ALU.mult, ALU.add), dv)
                y_evac[(e, tt, cs)] = t.count[dv]
            if e + 1 < NE:
                emit_wo_load(e + 1, cs)

    for tt in range(NT):
        t.wait(sync, dv, y_evac[(NE - 1, tt, 1)])
        t.inc(sync.dma_start(out=d_y[tt * P:(tt + 1) * P, :], in_=s_out[:, tt * C:(tt + 1) * C]),
              dOp[tt % 2], 16)

    if DEBUG_TAPS:
        t.wait(sync, av, t.count[av])
        t.wait(sync, dv, t.count[dv])
        t.inc(sync.dma_start(out=taps["tap_g"], in_=s_g[:]), dOp[0], 16)
        t.inc(sync.dma_start(out=taps["tap_x1"], in_=d_x1), dOp[0], 16)
        t.inc(sync.dma_start(out=taps["tap_S0"], in_=s_S0[:]), dOp[0], 16)

    t.wait(sync, dOp[0], t.count["dO0"])
    t.wait(sync, dOp[1], t.count["dO1"])

    stack.close()
    return nc


_PROG_CACHE = {}


def _get_program():
    key = ("prog", DEBUG_TAPS)
    if key not in _PROG_CACHE:
        _PROG_CACHE[key] = build_program()
    return _PROG_CACHE[key]


def make_in_maps(L0_attn_x, L0_attn_Wqkv, L0_attn_bqkv, L0_attn_Wout, L0_attn_bout,
                 L0_attn_slopes, L0_ffn_w_gate, L0_ffn_wi_0, L0_ffn_wi_1, L0_ffn_wo):
    f32 = np.float32
    x = np.ascontiguousarray(np.asarray(L0_attn_x, f32)).reshape(2, 4, KB, C)
    wqkv = np.ascontiguousarray(np.asarray(L0_attn_Wqkv, f32))
    bqkv = np.ascontiguousarray(np.asarray(L0_attn_bqkv, f32))
    wout = np.ascontiguousarray(np.asarray(L0_attn_Wout, f32))
    bout = np.ascontiguousarray(np.asarray(L0_attn_bout, f32))
    slopes = np.ascontiguousarray(np.asarray(L0_attn_slopes, f32))
    wg = np.asarray(L0_ffn_w_gate, f32)
    wi0 = np.asarray(L0_ffn_wi_0, f32)
    wi1 = np.asarray(L0_ffn_wi_1, f32)
    wo = np.asarray(L0_ffn_wo, f32)

    wg_im = np.ascontiguousarray(wg.reshape(NCH, P, NE).transpose(1, 0, 2).reshape(P, NCH * NE))
    ident = np.eye(P, dtype=np.float32).astype(ml_dtypes.bfloat16)
    jj = np.arange(P, dtype=f32)
    tri = np.ascontiguousarray((jj[None, :] >= jj[:, None]).astype(f32))
    cols = np.ascontiguousarray(
        np.stack([jj, P - jj, np.full(P, float(P), f32)], axis=1).astype(f32))
    tsc = (P - jj)[:, None] + P * (NT - 1 - np.arange(NT, dtype=f32))[None, :]
    tsc = np.ascontiguousarray(tsc.astype(f32))
    iota_row = jj[None, :]
    rows = np.concatenate([iota_row, np.zeros((1, P), f32)], axis=1).astype(f32)

    def wi_im(w):
        return np.ascontiguousarray(
            w.reshape(NCH, P, NHC, P).transpose(2, 1, 0, 3).reshape(M, C))

    blocks = []
    for e in range(NE):
        blocks.append(wi_im(wi0[e]))
        blocks.append(wi_im(wi1[e]))
        blocks.append(np.ascontiguousarray(wo[e]))
    wffn_im = np.ascontiguousarray(np.concatenate(blocks, axis=0))

    in_maps = []
    for r in range(8):
        b, s = r // 4, r % 4
        xp = np.zeros((NPT * P, C), f32)
        if s > 0:
            pref = x[b, 0:s].reshape(s * KB, C)
            xp[NPT * P - s * KB:] = pref
        nvalid = s * 8
        ptsc = np.full((P, NPT), 1e9, f32)
        for pt in range(NPT - nvalid, NPT):
            ptsc[:, pt] = (P - jj) + P * (NPT - 1 - pt)
        in_maps.append({
            "xs": np.ascontiguousarray(x[b, s]),
            "xp": xp,
            "wqkv": wqkv, "bqkv": bqkv, "wout": wout, "bout": bout,
            "slopes": slopes, "wg_im": wg_im, "wffn_im": wffn_im,
            "c_ident": ident, "c_tri": tri, "c_cols": cols, "c_tsc": tsc,
            "c_ptsc": np.ascontiguousarray(ptsc),
            "c_rows": rows,
        })
    return in_maps


def kernel(L0_attn_x, L0_attn_Wqkv, L0_attn_bqkv, L0_attn_Wout, L0_attn_bout,
           L0_attn_slopes, L0_ffn_w_gate, L0_ffn_wi_0, L0_ffn_wi_1, L0_ffn_wo):
    nc = _get_program()
    in_maps = make_in_maps(L0_attn_x, L0_attn_Wqkv, L0_attn_bqkv, L0_attn_Wout, L0_attn_bout,
                           L0_attn_slopes, L0_ffn_w_gate, L0_ffn_wi_0, L0_ffn_wi_1, L0_ffn_wo)
    res = run_bass_kernel_spmd(nc, in_maps, core_ids=list(range(8)))
    shards = [res.results[r]["y"] for r in range(8)]
    out = np.stack(shards, axis=0).reshape(2, 4 * KB, C).astype(np.float32)
    return out
